# version 33
# baseline (speedup 1.0000x reference)
import contextlib
import numpy as np
import concourse.bass as bass
import concourse.mybir as mybir
from concourse.bass_utils import run_bass_kernel_spmd

F32 = mybir.dt.float32
BF16 = mybir.dt.bfloat16
I32 = mybir.dt.int32
AF = mybir.ActivationFunctionType
ALU = mybir.AluOpType
AX = mybir.AxisListType

T = 4096
D = 1024
H = 16
NT = 32
NCH = 8
RMS_EPS = 1.1920929e-07
LN_EPS = 1e-5
ABS_EPS = 1e-6

DEBUG_STAGE = "full"
DEBUG_TAPS = ()

ENGINES = ("sync", "scalar", "vector", "tensor", "gpsimd")


class _Op:
    __slots__ = ("eng", "fn", "after", "is_dma", "needs_inc", "inc_val",
                 "tag", "strict", "fence", "xpose", "lane", "fence_deps")

    def __init__(self, eng, fn, after, is_dma, tag):
        self.eng, self.fn, self.after, self.is_dma, self.tag = eng, fn, after, is_dma, tag
        self.needs_inc = False
        self.inc_val = 0
        self.strict = ()
        self.fence = False
        self.xpose = False


class Prog:
    def __init__(self, nc):
        self.nc = nc
        self.ops = {e: [] for e in ENGINES}
        self.all_ops = []

    def emit(self, eng, fn, after=(), strict=(), tag=""):
        op = _Op(eng, fn, tuple(t for t in after if t is not None), False, tag)
        op.strict = tuple(t for t in strict if t is not None)
        self.ops[eng].append(op)
        self.all_ops.append(op)
        return op

    def dma(self, eng, fn, after=(), fence=False, xpose=False, tag=""):
        op = _Op(eng, fn, tuple(t for t in after if t is not None), True, tag)
        op.fence = fence or xpose
        op.xpose = xpose
        self.ops[eng].append(op)
        self.all_ops.append(op)
        return op

    def build(self):
        nc = self.nc
        NLANE = 8
        for op in self.all_ops:
            if op.is_dma:
                op.needs_inc = True
            for dep in op.strict:
                dep.needs_inc = True
            for dep in op.after:
                if dep.eng != op.eng or (dep.is_dma and not op.is_dma):
                    dep.needs_inc = True
        cum = {e: 0 for e in ENGINES}
        lane_cum = {}
        dma_idx = {e: 0 for e in ENGINES}
        all_lanes = {}
        prev_dmas = {e: [] for e in ENGINES}
        for op in self.all_ops:
            if op.is_dma:
                lane = dma_idx[op.eng] % NLANE
                op.lane = (op.eng, lane)
                dma_idx[op.eng] += 1
                if op.fence:
                    op.fence_deps = tuple(prev_dmas[op.eng])
                lane_cum[op.lane] = lane_cum.get(op.lane, 0) + 16
                op.inc_val = lane_cum[op.lane]
                lst = [d for d in prev_dmas[op.eng] if d.lane != op.lane]
                prev_dmas[op.eng] = lst + [op]
            else:
                if op.needs_inc:
                    cum[op.eng] += 1
                op.inc_val = cum[op.eng]
        self.sems = {}
        stack = contextlib.ExitStack()
        for e in ENGINES:
            if any((not op.is_dma) and op.needs_inc for op in self.ops[e]):
                self.sems[e] = stack.enter_context(nc.semaphore(f"sem_{e}"))
        for key in lane_cum:
            self.sems[key] = stack.enter_context(nc.semaphore(f"dsem_{key[0]}_{key[1]}"))
        block = stack.enter_context(nc.Block())
        for e in ENGINES:
            if not self.ops[e]:
                continue
            sec = getattr(block, e)

            def emit_stream(engine, ops_e=self.ops[e], e=e):
                waited = {}

                def wait(dep):
                    key = dep.lane if dep.is_dma else dep.eng
                    v = dep.inc_val
                    if waited.get(key, -1) < v:
                        engine.wait_ge(self.sems[key], v)
                        waited[key] = v

                for op in ops_e:
                    if op.fence:
                        for dep in getattr(op, "fence_deps", ()):
                            wait(dep)
                    for dep in op.strict:
                        wait(dep)
                    for dep in op.after:
                        if dep.eng == e and not dep.is_dma and not op.is_dma:
                            continue
                        wait(dep)
                    ins = op.fn(engine)
                    if op.is_dma:
                        ins.then_inc(self.sems[op.lane], 16)
                    elif op.needs_inc:
                        ins.then_inc(self.sems[e], 1)

            sec(emit_stream)
        return stack


class MK:
    def __init__(self, taps=()):
        self.taps = set(taps)
        self.nc = nc = bass.Bass("TRN2", target_bir_lowering=False)
        self.p = Prog(nc)
        self.dbg = {}

        dr = nc.dram_tensor
        self.x_h = dr("x", [T, D], F32, kind="ExternalInput").ap()
        for nm, sh in (("W_DQ", [512, 64]), ("W_UQ", [64, 512]), ("W_DKV", [1024, 64]),
                       ("W_UKV", [128, 1024]), ("W_QR", [64, 64]), ("W_KR", [64, 64]),
                       ("W_O", [64, 64]), ("q_norm_w", [64]), ("k_norm_w", [64]),
                       ("W_fc", [4096, 1024]), ("ln_w", [4096]), ("ln_b", [4096]),
                       ("W_proj", [1024, 4096])):
            setattr(self, nm, dr(nm, sh, F32, kind="ExternalInput").ap())
        self.out_h = dr("out", [T, D], F32, kind="ExternalOutput").ap()
        self.ut_h = dr("ut_scr", [D, T], BF16, kind="Internal").ap()
        self.x1_h = dr("x1_scr", [T, D], F32, kind="Internal").ap()
        self.wfcq_h = dr("wfcq_scr", [D, 4096], BF16, kind="Internal").ap()
        self.wprq_h = dr("wprq_scr", [4096, D], BF16, kind="Internal").ap()
        self.h_h = dr("h_scr", [T, 4096], BF16, kind="Internal").ap()

        A = lambda nm, sh, dt: nc.alloc_sbuf_tensor(nm, sh, dt).ap()
        self.identf = A("identf", [128, 128], F32)
        self.onesb = A("onesb", [1, 128], BF16)
        self.onesf = A("onesf", [1, 128], F32)
        self.wpa = A("wpa", [128, 384], BF16)
        self.lnw_c = A("lnw_c", [128, 32], F32)
        self.lnb_c = A("lnb_c", [128, 32], F32)
        self.lnb_b = A("lnb_b", [128, 32], BF16)
        self.lnbB = A("lnbB", [128, 4096], BF16)
        self.ybB_t = A("ybB_t", [128, 1040], F32)
        self.st1 = A("st1", [128, 256], F32)
        self.stp = A("stp", [128, 192], F32)
        self.stx = A("stx", [128, 96], F32)
        self.nsc = A("nsc", [128, 64], F32)
        self.nsci = A("nsci", [128, 64], I32)
        self.biasM = A("biasM", [128, 2], F32)
        self.sqsc = A("sqsc", [128, 2560], F32)
        self.rows = A("rows", [1, 512], F32)
        self.gst = A("gst", [128, 96], F32)

        self.pool = A("pool", [128, 32768], BF16)
        self.pool2 = A("pool2", [128, 12288], F32)
        self.pool3 = A("pool3", [128, 24576], BF16)

        P, P2, P3 = self.pool, self.pool2, self.pool3
        P2b = P2.bitcast(BF16)
        Pf = P.bitcast(F32)
        self.wb = P[:, 0:2368]
        self.q_st = P[:, 0:4096]
        self.k_st = P[:, 4096:8192]
        self.qT = P[:, 8192:12288]
        self.kT = P[:, 12288:16384]
        self.ln1c = [P[:, 16384:20480], P[:, 20480:24576]]
        self.ln1T = [P[:, 24576:28672], P[:, 28672:32768]]
        self.a_tok = P[:, 0:32768]
        self.ln3T = P[:, 0:32768]
        self.hT8 = [P[:, i * 4096:(i + 1) * 4096] for i in range(8)]
        self.outstP = P2[:, 8192:10240]
        self.x1s2_f = [P2[:, 10240:11264], P2[:, 11264:12288]]

        self.wsc = P2[:, 0:1536]
        self.wsc2 = P2[:, 1536:3072]
        self.x_stage = P2[:, 0:4096]
        self.x1slots = [P2[:, 4096:5120], P2[:, 7168:8192], P2[:, 10240:11264],
                        P2[:, 11264:12288], P2[:, 2048:3072], P2[:, 3072:4096],
                        P2[:, 8192:9216], P2[:, 9216:10240]]
        self.den16 = P2b[0:16, 20480:24576]
        self.lnw_row = P2[0:1, 4096:8192]
        self.qw_f32 = [P2[:, 4096:5120], P2[:, 5120:6144]]
        self.qw_ind = P2[:, 6144:7168]
        self.qw_sq = P2[:, 7168:8192]
        self.qw_sgn = P2b[:, 16384:17408]
        self.qw_q = P2b[:, 17408:18432]
        self.qw_tr = [P2b[:, 18432:19456], P2b[:, 19456:20480]]

        self.wst_fc = [P2b[:, 10240:14336], P2b[:, 14336:18432]]
        self.wst_pr = [P2b[:, 0:8192], P2b[:, 8192:16384]]

        self.cmask = A("cmask", [128, 2048], BF16)
        self.qT2 = A("qT2", [128, 4096], BF16)
        self.v_st2 = [P3[:, 0:2080], P3[:, 18944:21024]]
        self.v_st = P3[:, 0:2080]
        self.ebuf = P3[:, 3072:6144]
        self.uh = P3[:, 6144:10240]
        self.kT2 = P3[:, 10240:14336]
        self.qT_v = [self.qT, self.qT2]
        self.kT_v = [self.kT, self.kT2]
        self.denst = P3[64:65, 14336:18432]
        self.denT = P3[:, 18432:18944]
        self.hbuf = P3[:, 0:16384]
        self.ln3tok = P3[:, 16384:24576]
        self.ysb8 = P3.bitcast(F32)[:, 4096:12288]

        self.psA = nc.alloc_psum_tensor("psA", [128, 3584], F32).ap()
        self.psB = nc.alloc_psum_tensor("psB", [128, 512], F32).ap()
        self.sb_left = nc.sbuf_bytes_remaining

    def tap(self, name, shape, dt=F32):
        ap = self.nc.dram_tensor("dbg_" + name, shape, dt, kind="ExternalOutput").ap()
        self.dbg[name] = ap
        return ap

    def newton_rsqrt(self, dst, src, n, after, iters=3, tagp=""):
        p = self.p
        np_ = src.shape[0]
        i0, i1 = self.nsci[0:np_, 0:n], self.nsci[0:np_, 32:32 + n]
        tt = self.nsc[0:np_, 0:n]
        c1 = p.emit("vector", lambda e: e.tensor_copy(i0, src.bitcast(I32)), after=after,
                    strict=after, tag=tagp + "Nc")
        sh = p.emit("vector", lambda e: e.tensor_scalar(i1, i0, 1, None, ALU.arith_shift_right),
                    strict=[c1], tag=tagp + "Ns")
        mg = p.emit("vector", lambda e: e.tensor_scalar(i0, i1, -1.0, 1597463007.0, ALU.mult, ALU.add),
                    strict=[sh], tag=tagp + "Nm")
        prev = p.emit("vector", lambda e: e.tensor_copy(dst, i0.bitcast(F32)), strict=[mg],
                      tag=tagp + "Nb")
        for it in range(iters):
            a1 = p.emit("vector", lambda e: e.tensor_mul(tt, src, dst), strict=[prev], tag=tagp + f"N{it}a")
            a2 = p.emit("vector", lambda e: e.tensor_mul(tt, tt, dst), strict=[a1], tag=tagp + f"N{it}b")
            a3 = p.emit("vector", lambda e: e.tensor_scalar(tt, tt, -0.5, 1.5, ALU.mult, ALU.add),
                        strict=[a2], tag=tagp + f"N{it}c")
            prev = p.emit("vector", lambda e: e.tensor_mul(dst, dst, tt), strict=[a3], tag=tagp + f"N{it}d")
        return prev

    def lnstats_finalize(self, sumcol, sqcol, rstd, negmur, eps, after, tagp, n=D):
        p, st1 = self.p, self.st1
        w = sumcol.shape[1]
        mu = self.stx[:, 0:w]
        v_ = self.stx[:, 32:32 + w]
        m1 = p.emit("vector", lambda e: e.tensor_scalar(mu, sumcol, 1.0 / n, None, ALU.mult),
                    after=after, strict=after[-2:], tag=tagp + "mu")
        v1 = p.emit("vector", lambda e: e.tensor_mul(v_, mu, mu), strict=[m1], tag=tagp + "m2")
        v2 = p.emit("vector", lambda e: e.tensor_scalar(self.nsc[:, 32:32 + w], sqcol, 1.0 / n, eps,
                                                        ALU.mult, ALU.add), strict=[v1], tag=tagp + "e2")
        v3 = p.emit("vector", lambda e: e.tensor_sub(v_, self.nsc[:, 32:32 + w], v_), strict=[v2],
                    tag=tagp + "v")
        nr = self.newton_rsqrt(rstd, v_, w, [v3], tagp=tagp)
        nm = p.emit("vector", lambda e: e.tensor_mul(negmur, mu, rstd), strict=[nr], tag=tagp + "nm")
        return p.emit("vector", lambda e: e.tensor_scalar(negmur, negmur, -1.0, None, ALU.mult),
                      strict=[nm], tag=tagp + "nm2")

    def phase0(self):
        p = self.p
        t0 = p.emit("gpsimd", lambda e: e.memset(self.identf[:], 0.0), tag="id0")
        self.tok_id = p.emit("gpsimd", lambda e: e.affine_select(
            out=self.identf[:], in_=self.identf[:], compare_op=ALU.not_equal, fill=1.0,
            base=0, pattern=[[-1, 128]], channel_multiplier=1), strict=[t0], tag="id1")
        self.tok_ones = p.emit("gpsimd", lambda e: e.memset(self.onesb[0:1, :], 1.0), tag="onesb")
        self.tok_onesf = p.emit("gpsimd", lambda e: e.memset(self.onesf[0:1, :], 1.0), tag="onesf")
        self.tok_vone = p.emit("gpsimd", lambda e: e.memset(
            self.v_st.rearrange("p (m l) -> p m l", l=65)[:, :, 64:65], 1.0), tag="vone")
        tok_vone2 = p.emit("gpsimd", lambda e: e.memset(
            self.v_st2[1].rearrange("p (m l) -> p m l", l=65)[:, :, 64:65], 1.0),
            strict=[self.tok_vone], tag="vone2")
        self.tok_vone_v = [self.tok_vone, tok_vone2]
        cm0 = p.emit("gpsimd", lambda e: e.memset(self.cmask[:], 1.0), tag="cm0")
        cmts = [cm0]
        for rr in range(4):
            cmts.append(p.emit("gpsimd", lambda e, rr=rr: e.affine_select(
                out=self.cmask[:, rr * 512:(rr + 1) * 512],
                in_=self.cmask[:, rr * 512:(rr + 1) * 512],
                compare_op=ALU.is_ge, fill=0.0,
                base=-128 * rr, pattern=[[1, 512]], channel_multiplier=-1),
                strict=[cmts[-1]], tag=f"cm{rr}"))
        self.tok_cmask = cmts[-1]

        wsc, wsc2, wb, psA = self.wsc, self.wsc2, self.wb, self.psA
        dq = [p.dma("sync", lambda e, c=c: e.dma_start(
            out=wsc[:, c * 64:(c + 1) * 64], in_=self.W_DQ[c * 128:(c + 1) * 128, :]), tag=f"ldq{c}")
            for c in range(4)]
        uq = p.dma("sync", lambda e: e.dma_start(out=wsc[0:64, 256:768], in_=self.W_UQ[:]), tag="lduq")
        dkv = [p.dma("sync", lambda e, c=c: e.dma_start(
            out=wsc[:, 768 + c * 64:768 + (c + 1) * 64], in_=self.W_DKV[c * 128:(c + 1) * 128, :]),
            tag=f"ldkv{c}") for c in range(8)]
        ukv = p.dma("sync", lambda e: e.dma_start(out=wsc2[:, 0:1024], in_=self.W_UKV[:]), tag="ldukv")
        qr = p.dma("sync", lambda e: e.dma_start(out=wsc2[0:64, 1024:1088], in_=self.W_QR[:]), tag="ldqr")
        kr = p.dma("sync", lambda e: e.dma_start(out=wsc2[0:64, 1088:1152], in_=self.W_KR[:]), tag="ldkr")
        wo = p.dma("sync", lambda e: e.dma_start(out=wsc2[0:64, 1152:1216], in_=self.W_O[:]), tag="ldwo")
        qnw = p.dma("sync", lambda e: e.dma_start(
            out=wsc2[0:64, 1216:1217], in_=self.q_norm_w.rearrange("(a b) -> a b", b=1)), tag="ldqnw")
        knw = p.dma("sync", lambda e: e.dma_start(
            out=wsc2[0:64, 1217:1218], in_=self.k_norm_w.rearrange("(a b) -> a b", b=1)), tag="ldknw")

        c_dq = p.emit("vector", lambda e: e.tensor_copy(wb[:, 0:256], wsc[:, 0:256]), after=dq, tag="cdq")
        c_dkv = p.emit("vector", lambda e: e.tensor_copy(wb[:, 512:1024], wsc[:, 768:1280]),
                       after=dkv, tag="cdkv")
        tuq = [p.emit("tensor", lambda e, c=c: e.transpose(
            psA[:, c * 512:c * 512 + 64], wsc[0:64, 256 + c * 128:256 + (c + 1) * 128], self.identf[0:64, 0:64]),
            after=[uq, self.tok_id], tag=f"tuq{c}") for c in range(4)]
        e_uq = p.emit("vector", lambda e: e.tensor_copy(
            wb[:, 256:512].rearrange("p (c n) -> p c n", c=4),
            psA[:, 0:2048].rearrange("p (c n) -> p c n", c=4)[:, :, 0:64]),
            after=tuq, strict=[c_dkv], tag="euq")
        e_ukv = []
        for rnd in range(2):
            tks = [p.emit("tensor", lambda e, rnd=rnd, c=c: e.transpose(
                psA[:, c * 512:c * 512 + 128],
                wsc2[:, rnd * 512 + c * 128:rnd * 512 + (c + 1) * 128], self.identf[:]),
                after=[ukv, self.tok_id, e_uq if rnd == 0 else e_ukv[0]], tag=f"tukv{rnd}{c}")
                for c in range(4)]
            e_ukv.append(p.emit("vector", lambda e, rnd=rnd: e.tensor_copy(
                wb[:, 1024 + rnd * 512:1536 + rnd * 512].rearrange("p (c n) -> p c n", c=4),
                psA[:, 0:2048].rearrange("p (c n) -> p c n", c=4)[:, :, 0:128]),
                after=tks, strict=[e_uq], tag=f"eukv{rnd}"))
        tqr = p.emit("tensor", lambda e: e.transpose(psA[0:64, 0:64], wsc2[0:64, 1024:1088],
                                                     self.identf[0:64, 0:64]), after=[qr, e_ukv[1]], tag="tqr")
        tkr = p.emit("tensor", lambda e: e.transpose(psA[0:64, 512:576], wsc2[0:64, 1088:1152],
                                                     self.identf[0:64, 0:64]), after=[kr], tag="tkr")
        two = p.emit("tensor", lambda e: e.transpose(psA[0:64, 1024:1088], wsc2[0:64, 1152:1216],
                                                     self.identf[0:64, 0:64]), after=[wo], tag="two")
        e_qr = p.emit("vector", lambda e: e.tensor_copy(wb[0:64, 2048:2112], psA[0:64, 0:64]),
                      after=[tqr], strict=[e_ukv[1]], tag="eqr")
        e_kr = p.emit("vector", lambda e: e.tensor_copy(wb[0:64, 2112:2176], psA[0:64, 512:576]),
                      after=[tkr], tag="ekr")
        e_wo = p.emit("vector", lambda e: e.tensor_copy(wb[0:64, 2176:2240], psA[0:64, 1024:1088]),
                      after=[two], tag="ewo")
        mmAq = None
        for c in range(4):
            mmAq = p.emit("tensor", lambda e, c=c: e.matmul(
                psA[0:64, 2560:2624], wb[:, c * 64:(c + 1) * 64], wb[:, 256 + c * 64:256 + (c + 1) * 64],
                start=(c == 0), stop=(c == 3)), after=[c_dq, e_uq, two, e_qr, e_kr, e_wo],
                tag=f"mAq{c}")
        e_Aq = p.emit("vector", lambda e: e.tensor_copy(wsc[0:64, 1280:1344], psA[0:64, 2560:2624]),
                      after=[mmAq], strict=[e_wo], tag="eAq")
        mmAqT = None
        for c in range(4):
            mmAqT = p.emit("tensor", lambda e, c=c: e.matmul(
                psA[0:64, 3072:3136], wb[:, 256 + c * 64:256 + (c + 1) * 64], wb[:, c * 64:(c + 1) * 64],
                start=(c == 0), stop=(c == 3)), after=[mmAq], tag=f"mAqT{c}")
        e_AqT = p.emit("vector", lambda e: e.tensor_copy(wsc[0:64, 1344:1408], psA[0:64, 3072:3136]),
                       after=[mmAqT], strict=[e_Aq], tag="eAqT")
        mmAkv = None
        for c in range(8):
            mmAkv = p.emit("tensor", lambda e, c=c: e.matmul(
                psA[0:64, 2560:2688], wb[:, 512 + c * 64:512 + (c + 1) * 64],
                wb[:, 1024 + c * 128:1024 + (c + 1) * 128],
                start=(c == 0), stop=(c == 7)), after=[c_dkv, e_ukv[1], e_Aq], tag=f"mAkv{c}")
        e_Akv = p.emit("vector", lambda e: e.tensor_copy(wsc2[0:64, 1280:1408], psA[0:64, 2560:2688]),
                       after=[mmAkv], strict=[e_AqT], tag="eAkv")
        wqk = p.emit("vector", lambda e: e.tensor_mul(wsc2[0:64, 1218:1219], wsc2[0:64, 1216:1217],
                                                      wsc2[0:64, 1217:1218]),
                     after=[qnw, knw], strict=[e_Akv], tag="wqk")
        aqwT = p.emit("vector", lambda e: e.tensor_scalar(wsc[0:64, 1408:1472], wsc[0:64, 1344:1408],
                                                          wsc2[0:64, 1216:1217], None, ALU.mult),
                      after=[qnw], strict=[wqk], tag="aqwT")
        aq2T = p.emit("vector", lambda e: e.tensor_scalar(wsc[0:64, 1472:1536], wsc[0:64, 1344:1408],
                                                          wsc2[0:64, 1218:1219], None, ALU.mult),
                      strict=[aqwT], tag="aq2T")
        c_aqwT = p.emit("vector", lambda e: e.tensor_copy(wb[0:64, 2240:2304], wsc[0:64, 1408:1472]),
                        strict=[aq2T], tag="caqwT")
        tAq2 = p.emit("tensor", lambda e: e.transpose(psA[0:64, 3072:3136], wsc[0:64, 1472:1536],
                                                      self.identf[0:64, 0:64]), after=[aq2T, e_AqT, mmAkv],
                      tag="tAq2")
        aAq2 = p.emit("vector", lambda e: e.tensor_copy(self.wpa[0:64, 192:256], psA[0:64, 3072:3136]),
                      after=[tAq2], strict=[c_aqwT], tag="aAq2")
        tAvT = p.emit("tensor", lambda e: e.transpose(psA[0:64, 2560:2624], wsc2[0:64, 1344:1408],
                                                      self.identf[0:64, 0:64]), after=[e_Akv, tAq2], tag="tAvT")
        e_AvT = p.emit("vector", lambda e: e.tensor_copy(wb[0:64, 2304:2368], psA[0:64, 2560:2624]),
                       after=[tAvT], strict=[aAq2], tag="eAvT")
        mmAqr = p.emit("tensor", lambda e: e.matmul(psA[0:64, 3072:3136], wb[0:64, 2240:2304],
                                                    wb[0:64, 2048:2112], start=True, stop=True),
                       after=[c_aqwT, e_qr, aAq2, tAvT], tag="mAqr")
        aAqr = p.emit("vector", lambda e: e.tensor_copy(self.wpa[0:64, 256:320], psA[0:64, 3072:3136]),
                      after=[mmAqr], strict=[e_AvT], tag="aAqr")
        mmAvv = p.emit("tensor", lambda e: e.matmul(psA[0:64, 2560:2624], wb[0:64, 2304:2368],
                                                    wb[0:64, 2176:2240], start=True, stop=True),
                       after=[e_AvT, e_wo, mmAqr], tag="mAvv")
        aAvv = p.emit("vector", lambda e: e.tensor_copy(self.wpa[0:64, 320:384], psA[0:64, 2560:2624]),
                      after=[mmAvv], strict=[aAqr], tag="aAvv")
        asm = [
            p.emit("vector", lambda e: e.tensor_copy(self.wpa[0:64, 0:64], wsc[0:64, 1280:1344]),
                   after=[e_Aq], strict=[aAvv], tag="aAq"),
            p.emit("vector", lambda e: e.tensor_copy(self.wpa[0:64, 64:128], wsc2[0:64, 1280:1344]),
                   after=[e_Akv], tag="aAk"),
            p.emit("vector", lambda e: e.tensor_copy(self.wpa[0:64, 128:192], wb[0:64, 2112:2176]),
                   after=[e_kr], tag="aAkr"),
            aAq2, aAqr, aAvv,
        ]
        self.tok_wpa = p.dma("sync", lambda e: e.dma_start(out=self.wpa[64:128, :], in_=self.wpa[0:64, :]),
                             after=asm, tag="wpacp")
        if "setup" in self.taps:
            tw = self.tap("wpa", [128, 384], BF16)
            p.dma("sync", lambda e: e.dma_start(out=tw, in_=self.wpa[:]), after=[self.tok_wpa],
                  tag="tap_wpa")

    def phase1(self):
        p, st1, sqsc = self.p, self.st1, self.sqsc
        stats, last_sq, prev_by_buf = [], {}, {}
        for t in range(NT):
            buf = t % 2
            xs = self.x_stage[:, buf * 1024:(buf + 1) * 1024]
            dep = prev_by_buf[buf] if buf in prev_by_buf else [self.tok_wpa]
            ld = p.dma("sync", lambda e, t=t, xs=xs: e.dma_start(
                out=xs, in_=self.x_h[t * 128:(t + 1) * 128, :]), after=dep, tag=f"p1l{t}")
            sm = p.emit("vector", lambda e, t=t, xs=xs: e.reduce_sum(st1[:, t:t + 1], xs, axis=AX.X),
                        after=[ld], tag=f"p1s{t}")
            sq = p.emit("scalar", lambda e, t=t, xs=xs, buf=buf: e.activation(
                sqsc[:, buf * 1024:(buf + 1) * 1024], xs, AF.Square, accum_out=st1[:, 32 + t:33 + t]),
                after=[ld], strict=[last_sq[buf]] if buf in last_sq else [], tag=f"p1q{t}")
            last_sq[buf] = sq
            prev_by_buf[buf] = [sq, sm]
            stats += [sm, sq]
        self.tok_ln1 = self.lnstats_finalize(st1[:, 0:32], st1[:, 32:64], st1[:, 64:96],
                                             st1[:, 96:128], LN_EPS, stats, "p1")
        self.tok_p1sq = stats

    def chunk(self, c):
        p, st1 = self.p, self.st1
        buf = c % 2
        lc_v, lT_v = self.ln1c[buf], self.ln1T[buf]
        affs = []
        prev_half = None
        for hf in range(2):
            xs = self.x_stage[:, hf * 2048:(hf + 1) * 2048]
            dep = [self.tok_ln1]
            key = (c - 1, hf)
            if key in self.chunk_half_done:
                dep.append(self.chunk_half_done[key])
            ldc = p.dma("scalar", lambda e, c=c, hf=hf, xs=xs: e.dma_start(
                out=xs.rearrange("p (m l) -> p m l", l=128),
                in_=self.x_h.rearrange("(thi tlo) d -> tlo thi d", tlo=128)
                    [:, hf * 16:(hf + 1) * 16, c * 128:(c + 1) * 128]),
                after=dep, tag=f"c{c}x{hf}")
            for i in range(16):
                th = hf * 16 + i
                af = p.emit("vector", lambda e, i=i, th=th, xs=xs, lc_v=lc_v: e.tensor_scalar(
                    lc_v.rearrange("p (m l) -> p m l", l=128)[:, th, :],
                    xs.rearrange("p (m l) -> p m l", l=128)[:, i, :],
                    st1[:, 64 + th:65 + th], st1[:, 96 + th:97 + th], ALU.mult, ALU.add),
                    after=[ldc, self.tok_ln1] + (self.ln1c_users[buf] if th == 0 else []),
                    tag=f"c{c}a{th}")
                affs.append(af)
            self.chunk_half_done[(c, hf)] = affs[-1]
        trc = p.dma("sync", lambda e, lc_v=lc_v, lT_v=lT_v: e.dma_start(
            out=lT_v.rearrange("p (m l) -> p m l", l=128), in_=lc_v, transpose=True),
            after=affs + self.ln1T_users[buf], xpose=True, tag=f"c{c}tr")
        self.ln1c_users[buf] = [trc]
        self.ln1T_tok[buf] = trc
        self.ln1T_users[buf] = []
        if f"chunk{c}" in self.taps:
            tw = self.tap(f"ln1T{c}", [128, 4096], BF16)
            p.dma("sync", lambda e, tw=tw, lT_v=lT_v: e.dma_start(out=tw, in_=lT_v), after=[trc],
                  tag=f"tapc{c}")

    def prep_gen(self, h):
        p = self.p
        c, hh = h // 2, h % 2
        s0 = hh * 64
        buf = c % 2
        lT_v = self.ln1T[buf]
        psB = self.psB
        q_st, k_st = self.q_st, self.k_st
        v_st = self.v_st2[h % 2]
        stp, stx, nsc, rows = self.stp, self.stx, self.nsc, self.rows
        hp = self.hstate

        aqb = [self.sqsc[:, 0:64], self.sqsc[:, 64:128]]
        junk = self.sqsc[:, 128:256]
        kbuf = [self.sqsc[:, 256:384], self.sqsc[:, 384:512]]
        sqa_toks, sqq_toks, rdk_toks = [], [], []
        last_mm = last_ev3 = None
        for t_ in range(32):
            bu = t_ % 2
            dep = [self.ln1T_tok[buf], self.tok_wpa] + self.psB_chain
            mm = p.emit("tensor", lambda e, t_=t_, s0=s0, lT_v=lT_v: e.matmul(
                psB[:, 0:384], lT_v[s0:s0 + 64, t_ * 128:(t_ + 1) * 128],
                self.wpa[s0:s0 + 64, :], start=True, stop=True), after=dep, tag=f"h{h}p{t_}")
            last_mm = mm
            dep_ev = [mm] + ((hp.get("qk_free", []) + hp.get(f"vst_free{h % 2}", []))
                             if t_ == 0 else [])
            ev0 = p.emit("vector", lambda e, bu=bu: e.tensor_copy(
                aqb[bu], psB[:, 0:64]), after=dep_ev,
                strict=[sqa_toks[t_ - 2]] if t_ >= 2 else hp.get("prev_sqa", []),
                tag=f"h{h}ea{t_}")
            ev2 = p.emit("vector", lambda e, t_=t_: e.tensor_copy(
                k_st[:, t_ * 128:(t_ + 1) * 128], psB[:, 64:192]), after=dep_ev,
                tag=f"h{h}ek{t_}")
            ev1 = p.emit("vector", lambda e, t_=t_: e.tensor_copy(
                q_st[:, t_ * 128:(t_ + 1) * 128], psB[:, 192:320]), after=dep_ev,
                tag=f"h{h}eq{t_}")
            ev4 = p.emit("vector", lambda e, t_=t_, v_st=v_st: e.tensor_copy(
                v_st.rearrange("p (m l) -> p m l", l=65)[:, t_, 0:64], psB[:, 320:384]),
                after=dep_ev + [self.tok_vone_v[h % 2]], tag=f"h{h}ev{t_}")
            sqa = p.emit("scalar", lambda e, t_=t_, bu=bu: e.activation(
                junk[:, 0:64], aqb[bu], AF.Square, accum_out=stp[:, t_:t_ + 1]),
                after=[ev0] + (hp.get("stp_free", []) if t_ == 0 else []), tag=f"h{h}qa{t_}")
            sqa_toks.append(sqa)
            sqq = p.emit("scalar", lambda e, t_=t_: e.activation(
                junk, q_st[:, t_ * 128:(t_ + 1) * 128], AF.Square,
                accum_out=stp[:, 32 + t_:33 + t_]),
                after=[ev1], tag=f"h{h}qq{t_}")
            sqq_toks.append(sqq)
            sqk = p.emit("scalar", lambda e, t_=t_, bu=bu: e.activation(
                kbuf[bu], k_st[:, t_ * 128:(t_ + 1) * 128], AF.Square),
                after=[ev2], strict=[rdk_toks[t_ - 2]] if t_ >= 2 else hp.get("prev_rdk", []),
                tag=f"h{h}qk{t_}")
            rdk = p.emit("vector", lambda e, t_=t_, bu=bu: e.reduce_sum(
                stp[:, 64 + 2 * t_:66 + 2 * t_], kbuf[bu].rearrange("p (m l) -> p m l", l=64),
                axis=AX.X), after=[sqk],
                strict=hp.get("stp_free", []) if t_ == 0 else [], tag=f"h{h}rk{t_}")
            rdk_toks.append(rdk)
            self.psB_chain = [ev4]
            last_ev3 = ev4
            yield
        hp["prev_prep_sq"] = rdk_toks[-1]
        hp["prev_sqa"] = [sqa_toks[-1], sqa_toks[-2]]
        hp["prev_rdk"] = [rdk_toks[-1], rdk_toks[-2]]

        kview = stp[:, 64:128].rearrange("p (m l) -> p m l", l=2)
        stq = stp[:, 160:192]
        stk = stx[:, 64:96]
        tqa, tka = self.sqsc[:, 2496:2528], self.sqsc[:, 2528:2560]
        r2q = self.gst[:, 0:32]
        r2k = self.gst[:, 32:64]
        t1 = p.emit("vector", lambda e: e.tensor_scalar(tqa, stp[:, 0:32], 1.0 / 64, RMS_EPS,
                                                        ALU.mult, ALU.add),
                    after=[sqa_toks[-1], sqq_toks[-1], rdk_toks[-1]],
                    strict=[rdk_toks[-1]], tag=f"h{h}tq")
        t2 = p.emit("vector", lambda e: e.tensor_scalar(tka, kview[:, :, 0], 1.0 / 64, RMS_EPS,
                                                        ALU.mult, ALU.add), strict=[t1], tag=f"h{h}tk")
        g1 = p.emit("vector", lambda e: e.reciprocal(r2q, tqa), strict=[t2],
                    after=hp.get("gst_free", []), tag=f"h{h}rq")
        g2 = p.emit("vector", lambda e: e.reciprocal(r2k, tka), strict=[g1], tag=f"h{h}rk")
        nrq = self.newton_rsqrt(stq, tqa, 32, [g2], tagp=f"h{h}q")
        s125 = p.emit("vector", lambda e: e.tensor_scalar(stq, stq, 0.125, None, ALU.mult),
                      strict=[nrq], tag=f"h{h}q125")
        nrk = self.newton_rsqrt(stk, tka, 32, [s125], tagp=f"h{h}k")
        yield

        sc_toks = []
        for t_ in range(NT):
            a_ = p.emit("vector", lambda e, t_=t_: e.tensor_scalar(
                q_st.rearrange("p (m l) -> p m l", l=128)[:, t_, :],
                q_st.rearrange("p (m l) -> p m l", l=128)[:, t_, :],
                stq[:, t_:t_ + 1], None, ALU.mult),
                after=[s125], strict=[last_ev3] if t_ == 0 else [], tag=f"h{h}sq{t_}")
            b_ = p.emit("vector", lambda e, t_=t_: e.tensor_scalar(
                k_st.rearrange("p (m l) -> p m l", l=128)[:, t_, 0:64],
                k_st.rearrange("p (m l) -> p m l", l=128)[:, t_, 0:64],
                stk[:, t_:t_ + 1], None, ALU.mult),
                after=[nrk], tag=f"h{h}sk{t_}")
            sc_toks += [a_, b_]
            if t_ % 8 == 7:
                yield
        qT, kT = self.qT_v[h % 2], self.kT_v[h % 2]
        trq = p.dma("sync", lambda e, qT=qT: e.dma_start(
            out=qT.rearrange("p (m l) -> p m l", l=128), in_=q_st, transpose=True),
            after=sc_toks + hp.get(f"qT_free{h % 2}", []), xpose=True, tag=f"h{h}Tq")
        trk = p.dma("sync", lambda e, kT=kT: e.dma_start(
            out=kT.rearrange("p (m l) -> p m l", l=128), in_=k_st, transpose=True),
            after=sc_toks + hp.get(f"kT_free{h % 2}", []), xpose=True, tag=f"h{h}Tk")

        q2m = p.emit("vector", lambda e: e.tensor_mul(nsc[:, 0:32], stp[:, 32:64], r2q),
                     strict=[nrk], tag=f"h{h}q2b")
        k2s = p.emit("vector", lambda e: e.tensor_mul(nsc[:, 32:64], kview[:, :, 0], r2k),
                     strict=[q2m], tag=f"h{h}k2a")
        k2t = p.emit("vector", lambda e: e.tensor_add(nsc[:, 32:64], nsc[:, 32:64], kview[:, :, 1]),
                     strict=[k2s], tag=f"h{h}k2b")
        mq = p.emit("vector", lambda e: e.reduce_max(nsc[:, 62:63], nsc[:, 0:32], axis=AX.X),
                    strict=[k2t], tag=f"h{h}mq")
        mk = p.emit("vector", lambda e: e.reduce_max(nsc[:, 63:64], nsc[:, 32:64], axis=AX.X),
                    strict=[mq], tag=f"h{h}mk")
        pr = p.emit("vector", lambda e: e.tensor_mul(nsc[:, 62:63], nsc[:, 62:63], nsc[:, 63:64]),
                    strict=[mk], tag=f"h{h}pp")
        pr2 = p.emit("vector", lambda e: e.tensor_scalar(nsc[:, 62:63], nsc[:, 62:63], 0.015625, 1e-30,
                                                         ALU.mult, ALU.add), strict=[pr], tag=f"h{h}p2")
        trm = p.emit("tensor", lambda e: e.transpose(psB[0:1, 0:128], nsc[:, 62:63], self.identf[:]),
                     after=[pr2, self.tok_id] + self.psB_chain, tag=f"h{h}tm")
        evm = p.emit("vector", lambda e: e.tensor_copy(rows[0:1, 0:128], psB[0:1, 0:128]),
                     after=[trm], strict=[pr2] + hp.get("rows_free", []), tag=f"h{h}em")
        rmx = p.emit("vector", lambda e: e.reduce_max(rows[0:1, 128:129], rows[0:1, 0:128], axis=AX.X),
                     strict=[evm], tag=f"h{h}rm")
        nrm = self.newton_rsqrt(rows[0:1, 129:130], rows[0:1, 128:129], 1, [rmx], tagp=f"h{h}M")
        msq = p.emit("vector", lambda e: e.tensor_mul(rows[0:1, 130:131], rows[0:1, 128:129],
                                                      rows[0:1, 129:130]), strict=[nrm], tag=f"h{h}ms")
        mng = p.emit("vector", lambda e: e.tensor_scalar(rows[0:1, 131:132], rows[0:1, 130:131], -1.0,
                                                         None, ALU.mult), strict=[msq], tag=f"h{h}mg")
        bcm = p.emit("tensor", lambda e: e.matmul(psB[:, 128:129], self.onesf[0:1, 0:128],
                                                  rows[0:1, 131:132], start=True, stop=True),
                     after=[mng, self.tok_onesf, trm], tag=f"h{h}bc")
        evbm = p.emit("vector", lambda e, h=h: e.tensor_copy(self.biasM[:, h % 2:h % 2 + 1],
                                                             psB[:, 128:129]),
                      after=[bcm] + hp.get(f"biasM_free{h % 2}", []), strict=[mng], tag=f"h{h}eb")
        self.psB_chain = [evbm]
        hp["rows_free"] = [mng]
        hp["gst_free"] = [k2s]
        hp["stp_free"] = [k2t]
        hp["qk_free"] = [trq, trk]
        hp[f"prep{h}"] = (trq, trk, evbm)
        self.ln1T_users[buf] += [last_mm]
        yield

    def qc_gen(self, h):
        p = self.p
        c, hh = h // 2, h % 2
        s0 = hh * 64
        psA = self.psA
        qT, kT = self.qT_v[h % 2], self.kT_v[h % 2]
        v_st = self.v_st2[h % 2]
        ebuf = self.ebuf
        hp = self.hstate
        trq, trk, evbm = hp.pop(f"prep{h}")
        grp_exp = hp.setdefault("grp_exp", {0: None, 1: None})
        grp_av = hp.setdefault("grp_av", {0: None, 1: None})
        gidx = hp.get("gidx", 0)
        uev_prev, dev_prev = None, None
        last_exp = last_av = None
        pend = None

        def emit_av(pend):
            kts, gbuf, ex, mtoks, pnkt = pend
            ctx_l = None
            for i, kt in enumerate(kts):
                dep = [ex] + mtoks
                if kt == 0:
                    dep += [t for t in (hp.get("uev_prev"), hp.get("dev_prev")) if t is not None]
                    dep += [self.tok_vone_v[h % 2]] + hp.pop("ctx_init", [])
                ctx_l = p.emit("tensor", lambda e, i=i, kt=kt, gbuf=gbuf,
                               s=(kt == 0), st=(kt == pnkt - 1): e.matmul(
                    self.psA[0:65, 3072:3584],
                    v_st.rearrange("p (m l) -> p m l", l=65)[:, kt, :],
                    ebuf[:, gbuf * 1536 + i * 512:gbuf * 1536 + (i + 1) * 512],
                    start=s, stop=st), after=dep, tag=f"h{h}a{kt}")
            grp_av[gbuf] = ctx_l
            return ctx_l

        for qc in range(8):
            nkt = 4 * qc + 4
            groups = [list(range(g0, min(g0 + 3, nkt))) for g0 in range(0, nkt, 3)]
            ctx_last = None
            for kts in groups:
                gbuf = gidx % 2
                gidx += 1
                base = gbuf * 1536
                smms = []
                for i, kt in enumerate(kts):
                    dep = [trq, trk]
                    if i == 0 and grp_exp[gbuf] is not None:
                        dep.append(grp_exp[gbuf])
                    elif i == 0:
                        dep.append(self.tok_lncols)
                    if i == 0 and grp_av[gbuf] is not None:
                        dep.append(grp_av[gbuf])
                    smms.append(p.emit("tensor", lambda e, kt=kt, i=i, base=base, qc=qc: e.matmul(
                        psA[:, base + i * 512:base + (i + 1) * 512],
                        kT[:, kt * 128:(kt + 1) * 128], qT[:, qc * 512:(qc + 1) * 512],
                        start=True, stop=True), after=dep, tag=f"h{h}s{qc}_{kt}"))
                edep = smms + [evbm]
                if grp_av[gbuf] is not None:
                    edep.append(grp_av[gbuf])
                ex = p.emit("scalar", lambda e, n=len(kts), base=base, gbuf=gbuf, h=h: e.activation(
                    ebuf[:, gbuf * 1536:gbuf * 1536 + n * 512], psA[:, base:base + n * 512],
                    AF.Exp, bias=self.biasM[:, h % 2:h % 2 + 1]), after=edep, tag=f"h{h}x{qc}")
                grp_exp[gbuf] = ex
                last_exp = ex
                mtoks = []
                for i, kt in enumerate(kts):
                    if kt >= 4 * qc:
                        rr = kt - 4 * qc
                        mtoks.append(p.emit("vector", lambda e, i=i, rr=rr, gbuf=gbuf: e.tensor_mul(
                            ebuf[:, gbuf * 1536 + i * 512:gbuf * 1536 + (i + 1) * 512],
                            ebuf[:, gbuf * 1536 + i * 512:gbuf * 1536 + (i + 1) * 512],
                            self.cmask[:, rr * 512:(rr + 1) * 512]),
                            after=[ex, self.tok_cmask], tag=f"h{h}k{qc}_{kt}"))
                if pend is not None:
                    ctx_last = emit_av(pend)
                pend = (kts, gbuf, ex, mtoks, nkt)
                if kts[-1] == nkt - 1:
                    ctx_last = emit_av(pend)
                    pend = None
                last_av = ctx_last if ctx_last is not None else last_av
                yield
            uev = p.emit("vector", lambda e, qc=qc: e.tensor_copy(
                self.uh[0:65, qc * 512:(qc + 1) * 512], self.psA[0:65, 3072:3584]),
                after=[ctx_last] + ([hp.get("uh_dma"), hp.get("den_dma")] if qc == 0 else []),
                tag=f"h{h}u{qc}")
            hp["uev_prev"] = uev
            uev_prev = uev
            yield
        hp["gidx"] = gidx
        ust = p.dma("sync", lambda e, c=c, s0=s0: e.dma_start(
            out=self.ut_h[c * 128 + s0:c * 128 + s0 + 64, :],
            in_=self.uh[0:64, 0:4096]), after=[uev_prev], tag=f"h{h}U")
        dend = p.dma("sync", lambda e, h=h: e.dma_start(
            out=self.den16[h:h + 1, 0:4096], in_=self.uh[64:65, 0:4096]), after=[uev_prev],
            tag=f"h{h}D")
        hp["uh_dma"] = ust
        hp["den_dma"] = dend
        self.den_toks.append(dend)
        hp[f"qT_free{h % 2}"] = [last_av]
        hp[f"kT_free{h % 2}"] = [last_av]
        hp[f"vst_free{h % 2}"] = [last_av]
        hp[f"biasM_free{h % 2}"] = [last_exp]
        self.ust_toks.append(ust)
        self.tok_last_av = last_av
        self.tok_last_exp = last_exp
        if f"head{h}" in self.taps:
            tw = self.tap(f"uh{h}", [64, 4096], BF16)
            p.dma("sync", lambda e, tw=tw: e.dma_start(
                out=tw, in_=self.uh[0:64, 0:4096]), after=[uev_prev], tag=f"tapu{h}")

    def phase4_gen(self):
        p, st1, sqsc, a_tok = self.p, self.st1, self.sqsc, self.a_tok
        self.wld0 = p.dma("scalar", lambda e: e.dma_start(
            out=self.wst_fc[0].rearrange("p (dh j) -> p dh j", dh=8),
            in_=self.wfcq_h.rearrange("(dh dl) j -> dl dh j", dl=128)[:, :, 0:512]),
            after=self.tok_quant, tag="Fw0pre")
        trden = p.dma("sync", lambda e: e.dma_start(
            out=self.denT.rearrange("p (m l) -> p m l", l=16), in_=self.den16[:, 0:4096],
            transpose=True), after=self.den_toks, xpose=True, tag="p4td")
        wait_pool = self.ust_toks + self.hstate["qk_free"] + [self.tok_last_av] + \
            self.ln1c_users[0] + self.ln1c_users[1] + [self.tok_last_exp]
        atr = []
        for c in range(NCH):
            atr.append(p.dma("sync", lambda e, c=c: e.dma_start(
                out=a_tok.rearrange("p (m l) -> p m l", l=1024)[:, :, c * 128:(c + 1) * 128],
                in_=self.ut_h[c * 128:(c + 1) * 128, :], transpose=True),
                after=wait_pool, xpose=True, tag=f"p4a{c}"))
        rc_all = p.emit("vector", lambda e: e.reciprocal(sqsc[:, 2048:2560], self.denT[:, 0:512]),
                        after=[trden, self.hstate["prev_prep_sq"]], tag="p4rc")
        rcv = sqsc[:, 2048:2560].rearrange("p (m l) -> p m l", l=16)
        atv = a_tok.rearrange("p (m l) -> p m l", l=1024)
        self.x1_store = []
        self.tr_toks = []
        prev_sq = None
        prev_fin = [rc_all]
        prev_af = None
        ln3stats = {}
        aff3 = {}
        yield
        for g in range(4):
            stats = []
            for t_ in range(8 * g, 8 * g + 8):
                norm = None
                for hh in range(H):
                    norm = p.emit("vector", lambda e, t_=t_, hh=hh: e.tensor_scalar(
                        atv[:, t_, hh * 64:(hh + 1) * 64], atv[:, t_, hh * 64:(hh + 1) * 64],
                        rcv[:, t_, hh:hh + 1], None, ALU.mult),
                        after=atr + [rc_all], tag=f"p4n{t_}_{hh}")
                sm = p.emit("vector", lambda e, t_=t_: e.reduce_sum(
                    st1[:, 128 + t_:129 + t_], atv[:, t_, :], axis=AX.X),
                    after=[norm], tag=f"p4s{t_}")
                sq = p.emit("scalar", lambda e, t_=t_: e.activation(
                    sqsc[:, (t_ % 2) * 1024:(t_ % 2) * 1024 + 1024], atv[:, t_, :], AF.Square,
                    accum_out=st1[:, 160 + t_:161 + t_]),
                    after=[norm], strict=([prev_sq] if prev_sq else []) +
                    [self.hstate["prev_prep_sq"]], tag=f"p4q{t_}")
                prev_sq = sq
                stats += [sm, sq]
            fin2 = self.lnstats_finalize(st1[:, 128 + 8 * g:136 + 8 * g],
                                         st1[:, 160 + 8 * g:168 + 8 * g],
                                         st1[:, 64 + 8 * g:72 + 8 * g],
                                         st1[:, 96 + 8 * g:104 + 8 * g],
                                         LN_EPS, stats + prev_fin, f"p4f{g}")
            yield
            for t_ in range(8 * g, 8 * g + 8):
                bu = t_ % 2
                th = t_ % 8
                sl = self.x1slots[th]
                lx = p.dma("gpsimd", lambda e, t_=t_, sl=sl: e.dma_start(
                    out=sl, in_=self.x_h[t_ * 128:(t_ + 1) * 128, :]),
                    after=(self.tok_quant + [trden, rc_all,
                                             self.chunk_half_done[(NCH - 1, 1)],
                                             self.chunk_half_done[(NCH - 2, 1)]])
                    if t_ < 8 else
                    [aff3[t_ - 8], self.x1_store[t_ - 8], ln3stats[t_ - 8]],
                    tag=f"p4x{t_}")
                af = p.emit("vector", lambda e, t_=t_, bu=bu: e.tensor_scalar(
                    sqsc[:, bu * 1024:bu * 1024 + 1024], atv[:, t_, :],
                    st1[:, 64 + t_:65 + t_], st1[:, 96 + t_:97 + t_], ALU.mult, ALU.add),
                    after=[fin2, prev_sq], strict=[], tag=f"p4f{t_}")
                prev_af = af
                ad = p.emit("vector", lambda e, t_=t_, bu=bu, sl=sl: e.tensor_add(
                    sl, sl, sqsc[:, bu * 1024:bu * 1024 + 1024]),
                    after=[lx], strict=[af], tag=f"p4d{t_}")
                stx_ = p.dma("sync", lambda e, t_=t_, sl=sl: e.dma_start(
                    out=self.x1_h[t_ * 128:(t_ + 1) * 128, :], in_=sl),
                    after=[ad], tag=f"p4w{t_}")
                self.x1_store.append(stx_)
                sm3 = p.emit("vector", lambda e, t_=t_, sl=sl: e.reduce_sum(
                    st1[:, t_:t_ + 1], sl, axis=AX.X),
                    strict=[ad], tag=f"p4l{t_}")
                sq3 = p.emit("scalar", lambda e, t_=t_, sl=sl: e.activation(
                    sqsc[:, ((t_ + 1) % 2) * 1024:((t_ + 1) % 2) * 1024 + 1024], sl,
                    AF.Square, accum_out=st1[:, 32 + t_:33 + t_]),
                    after=[ad], tag=f"p4m{t_}")
                ln3stats[t_] = sq3
                prev_sq = sq3
                last_sm3 = sm3
            fin3 = self.lnstats_finalize(st1[:, 8 * g:8 + 8 * g], st1[:, 32 + 8 * g:40 + 8 * g],
                                         st1[:, 192 + 8 * g:200 + 8 * g],
                                         st1[:, 224 + 8 * g:232 + 8 * g],
                                         LN_EPS, [last_sm3, ln3stats[8 * g + 7],
                                                  ln3stats[8 * g + 6],
                                                  prev_af, fin2], f"p4g{g}")
            prev_fin = [fin3]
            yield
            for t in range(8 * g, 8 * g + 8):
                th = t % 8
                sl = self.x1slots[th]
                dep = [fin3]
                if th == 0 and self.tr_toks:
                    dep.append(self.tr_toks[-1])
                af3 = p.emit("vector", lambda e, t=t, th=th, sl=sl: e.tensor_scalar(
                    self.ln3tok[:, th * 1024:(th + 1) * 1024], sl,
                    st1[:, 192 + t:193 + t], st1[:, 224 + t:225 + t], ALU.mult, ALU.add),
                    after=dep, strict=[], tag=f"La{t}")
                aff3[t] = af3
            trdep = [aff3[8 * g + 6], aff3[8 * g + 7]]
            tr = p.dma("sync", lambda e, g=g: e.dma_start(
                out=self.ln3T.rearrange("p (m l) -> p m l", l=128)
                    .rearrange("p (g m) l -> p g m l", g=4)[:, g],
                in_=self.ln3tok[:, 0:8192], transpose=True),
                after=trdep, xpose=True, tag=f"Ltr{g}")
            self.tr_toks.append(tr)
            yield
        self.p4_last = [prev_af, prev_sq]
        self.aff3_last = [aff3[30], aff3[31]]
        if "x1" in self.taps:
            tw = self.tap("x1", [T, D], F32)
            p.dma("sync", lambda e, tw=tw: e.dma_start(out=tw, in_=self.x1_h[:]),
                  after=self.x1_store, tag="tapx1")

    def quant_prep(self):
        p, psA, identf = self.p, self.psA, self.identf
        lw = p.dma("sync", lambda e: e.dma_start(out=self.lnw_row,
                                                 in_=self.ln_w.rearrange("(a b) -> a b", a=1)),
                   after=[self.tok_wpa], tag="q_lw")
        prev_tt = {}

        def row_to_cols(row_tok, dstc, tagp):
            toks = []
            for g in range(32):
                bank = g % 4
                dep = [row_tok, self.tok_id]
                if (tagp, bank) in prev_tt:
                    dep.append(prev_tt[(tagp, bank)])
                tt = p.emit("tensor", lambda e, g=g, bank=bank: e.transpose(
                    psA[:, bank * 512:bank * 512 + 1], self.lnw_row[0:1, g * 128:(g + 1) * 128],
                    identf[0:1, 0:1]), after=dep, tag=f"{tagp}t{g}")
                ev = p.emit("vector", lambda e, g=g, bank=bank, dstc=dstc: e.tensor_copy(
                    dstc[:, g:g + 1], psA[:, bank * 512:bank * 512 + 1]), after=[tt],
                    strict=toks[-1:], tag=f"{tagp}e{g}")
                prev_tt[(tagp, bank)] = ev
                toks.append(ev)
            return toks

        lnw_toks = row_to_cols(lw, self.lnw_c, "qlw")
        lb = p.dma("sync", lambda e: e.dma_start(out=self.lnw_row,
                                                 in_=self.ln_b.rearrange("(a b) -> a b", a=1)),
                   after=lnw_toks, tag="q_lb")
        lnb_toks = row_to_cols(lb, self.lnb_c, "qlb")
        self.tok_lncols = lnb_toks[-1]
        self.tok_lb = lb

    def quant_units(self):
        p, psB, identf = self.p, self.psB, self.identf
        seg_toks = []
        for gsg in range(8):
            cs = p.emit("vector", lambda e, gsg=gsg: e.tensor_copy(
                self.qw_sgn[0:1, 0:512], self.lnw_row[0:1, gsg * 512:(gsg + 1) * 512]),
                after=[self.tok_lb, self.tok_lncols] + seg_toks[-1:], tag=f"qbs{gsg}")
            bc = p.emit("tensor", lambda e: e.matmul(
                psB[:, 0:512], self.onesb[0:1, 0:128], self.qw_sgn[0:1, 0:512],
                start=True, stop=True),
                after=[cs, self.tok_ones] + seg_toks[-1:] + self.psB_chain, tag=f"qbb{gsg}")
            ev = p.emit("vector", lambda e, gsg=gsg: e.tensor_copy(
                self.lnbB[:, gsg * 512:(gsg + 1) * 512], psB[:, 0:512]), after=[bc],
                strict=[cs], tag=f"qbe{gsg}")
            seg_toks.append(ev)
            self.psB_chain = [ev]
        yield
        marks = {}
        for widx, (w_hbm, nr_, nc2) in enumerate(((self.W_fc, 4096, 1024),
                                                  (self.W_proj, 1024, 4096))):
            if widx == 0:
                sview4 = None
                sview = w_hbm.rearrange("(s p) n -> s p n", p=128)
            else:
                sview4 = w_hbm.rearrange("(dc p) (jq n) -> dc jq p n", p=128, n=1024)
                sview = None
            acc = self.gst[:, 64 + widx:65 + widx]
            tmps = [self.gst[:, 66:67], self.gst[:, 67:68]]
            hcol = self.gst[:, 68 + widx:69 + widx]
            acc_tok = None
            red_prev = None
            use_prev = {0: None, 1: None}
            for s in range(32):
                bu = s % 2
                dst = self.qw_f32[bu]
                srcap = sview[s] if sview is not None else sview4[s // 4, s % 4]
                ld = p.dma("gpsimd", lambda e, dst=dst, srcap=srcap: e.dma_start(
                    out=dst, in_=srcap),
                    after=[use_prev[bu]] if use_prev[bu] is not None else
                    [self.tok_lncols, seg_toks[-1]],
                    tag=f"g{widx}l{s}")
                rd = p.emit("vector", lambda e, dst=dst, tmp=tmps[bu]: e.reduce_sum(
                    tmp, dst, axis=AX.X, apply_absolute_value=True),
                    after=[ld], strict=[red_prev] if red_prev else [], tag=f"g{widx}r{s}")
                if s == 0:
                    acc_tok = p.emit("vector", lambda e, tmp=tmps[bu], acc=acc: e.tensor_copy(
                        acc, tmp), strict=[rd], tag=f"g{widx}a{s}")
                else:
                    acc_tok = p.emit("vector", lambda e, tmp=tmps[bu], acc=acc: e.tensor_add(
                        acc, acc, tmp), strict=[rd, acc_tok], tag=f"g{widx}a{s}")
                red_prev = rd
                use_prev[bu] = rd
                if s % 4 == 3:
                    yield
            trg = p.emit("tensor", lambda e, acc=acc: e.transpose(psB[0:1, 0:128], acc,
                                                                  identf[:]),
                         after=[acc_tok, self.tok_id, seg_toks[-1]] + self.psB_chain,
                         tag=f"g{widx}t")
            evg = p.emit("vector", lambda e: e.tensor_copy(self.rows[0:1, 256:384], psB[0:1, 0:128]),
                         after=[trg], strict=[acc_tok], tag=f"g{widx}e")
            rs = p.emit("vector", lambda e: e.reduce_sum(self.rows[0:1, 384:385],
                                                         self.rows[0:1, 256:384], axis=AX.X),
                        strict=[evg], tag=f"g{widx}s")
            nel = float(nr_ * nc2)
            h1 = p.emit("vector", lambda e, nel=nel: e.tensor_scalar(
                self.rows[0:1, 385:386], self.rows[0:1, 384:385], 0.5 / nel, 0.5 * ABS_EPS,
                ALU.mult, ALU.add), strict=[rs], tag=f"g{widx}h")
            bch = p.emit("tensor", lambda e: e.matmul(psB[:, 128:129], self.onesf[0:1, 0:128],
                                                      self.rows[0:1, 385:386], start=True,
                                                      stop=True),
                         after=[h1, self.tok_onesf, trg], tag=f"g{widx}b")
            evh = p.emit("vector", lambda e, hcol=hcol: e.tensor_copy(hcol, psB[:, 128:129]),
                         after=[bch], strict=[h1], tag=f"g{widx}eh")
            hncol = self.gst[:, 70 + widx:71 + widx]
            nh = p.emit("vector", lambda e, hcol=hcol, hncol=hncol: e.tensor_scalar(
                hncol, hcol, -1.0, None, ALU.mult), strict=[evh], tag=f"g{widx}nh")
            self.psB_chain = [evg, evh]
            yield
            sq_prev = None
            ya_prev = None
            wq_extra = []
            for s in range(32):
                bu = s % 2
                dst = self.qw_f32[bu]
                srcap = sview[s] if sview is not None else sview4[s // 4, s % 4]
                ld = p.dma("gpsimd", lambda e, dst=dst, srcap=srcap: e.dma_start(
                    out=dst, in_=srcap),
                    after=[evh] if s < 2 else [marks[(widx, s - 2)]], tag=f"t{widx}l{s}")
                d1 = p.emit("vector", lambda e, dst=dst, hcol=hcol: e.tensor_scalar(
                    self.qw_ind[:, 0:1024], dst, hcol, None, ALU.is_gt),
                    after=[ld, nh], strict=[sq_prev] if sq_prev else [], tag=f"t{widx}in{s}")
                d2 = p.emit("vector", lambda e, dst=dst, hncol=hncol: e.tensor_scalar(
                    self.qw_sq[:, 0:1024], dst, hncol, None, ALU.is_lt),
                    strict=[d1], tag=f"t{widx}il{s}")
                wq = p.emit("vector", lambda e: e.tensor_sub(self.qw_q[:, 0:1024],
                                                             self.qw_ind[:, 0:1024],
                                                             self.qw_sq[:, 0:1024]),
                            after=wq_extra, strict=[d2], tag=f"t{widx}wq{s}")
                sq_prev = wq
                trb = self.qw_tr[bu]
                trw = p.dma("sync", lambda e, trb=trb: e.dma_start(
                    out=trb.rearrange("p (m l) -> p m l", l=128), in_=self.qw_q[:, 0:1024],
                    transpose=True),
                    after=[wq] + ([marks[(widx, s - 2)]] if s >= 2 else []), xpose=True,
                    tag=f"t{widx}tr{s}")
                wq_extra = [trw]
                if widx == 0:
                    stw = p.dma("gpsimd", lambda e, s=s, trb=trb: e.dma_start(
                        out=self.wfcq_h.rearrange("(dh dl) (sj jl) -> dl dh sj jl",
                                                  dl=128, jl=128)[:, :, s, :],
                        in_=trb.rearrange("p (dh jl) -> p dh jl", jl=128)), after=[trw],
                        tag=f"t{widx}st{s}")
                else:
                    dc, jq = s // 4, s % 4
                    ybt = self.ybB_t[:, 1032 + bu:1033 + bu]
                    stt = p.emit("vector", lambda e, jq=jq, ybt=ybt: e.scalar_tensor_tensor(
                        self.qw_sq[:, 0:512].bitcast(BF16)[:, 0:1024], self.qw_q[:, 0:1024],
                        1.0, self.lnbB[:, jq * 1024:(jq + 1) * 1024], ALU.mult, ALU.mult,
                        accum_out=ybt), after=[seg_toks[-1]],
                        strict=[wq] + ([ya_prev] if ya_prev else []), tag=f"yb{s}")
                    wq_extra.append(stt)
                    ya = p.emit("vector", lambda e, dc=dc, ybt=ybt, init=(jq == 0), s=s:
                                (e.tensor_copy(self.ybB_t[:, 1024 + dc:1025 + dc], ybt)
                                 if init else
                                 e.tensor_add(self.ybB_t[:, 1024 + dc:1025 + dc],
                                              self.ybB_t[:, 1024 + dc:1025 + dc], ybt)),
                                strict=[stt], tag=f"yba{s}")
                    ya_prev = ya
                    folds = []
                    for j_hi in range(8):
                        col = jq * 8 + j_hi
                        folds.append(p.emit("vector", lambda e, col=col, j_hi=j_hi, trb=trb:
                                            e.tensor_scalar(
                            trb[:, j_hi * 128:(j_hi + 1) * 128],
                            trb[:, j_hi * 128:(j_hi + 1) * 128],
                            self.lnw_c[:, col:col + 1], None, ALU.mult),
                            after=[trw, self.tok_lncols],
                            strict=[ya] + (folds[-1:] if folds else []),
                            tag=f"fw{s}_{j_hi}"))
                    stw = p.dma("gpsimd", lambda e, dc=dc, jq=jq, trb=trb: e.dma_start(
                        out=self.wprq_h.rearrange("(jq jh jl) (dc dl) -> jq dc jl jh dl",
                                                  jl=128, dl=128, jh=8)[jq, dc],
                        in_=trb.rearrange("p (jh dl) -> p jh dl", dl=128)),
                        after=folds, tag=f"t{widx}st{s}")
                marks[(widx, s)] = stw
                if s % 2 == 1:
                    yield
        ybdone = []
        for dc in range(8):
            trp = p.emit("tensor", lambda e, dc=dc: e.transpose(
                psB[0:1, 256:384], self.ybB_t[:, 1024 + dc:1025 + dc], identf[:]),
                after=[ya_prev, self.tok_id] + ybdone[-1:] + self.psB_chain, tag=f"ybt{dc}")
            evr = p.emit("vector", lambda e: e.tensor_copy(self.rows[0:1, 384:512],
                                                           psB[0:1, 256:384]),
                         after=[trp], strict=[ya_prev] + ybdone[-1:], tag=f"ybe{dc}")
            bcp = p.emit("tensor", lambda e: e.matmul(psB[:, 384:512], self.onesf[0:1, 0:128],
                                                      self.rows[0:1, 384:512], start=True,
                                                      stop=True),
                         after=[evr, self.tok_onesf], tag=f"ybb{dc}")
            evo = p.emit("vector", lambda e, dc=dc: e.tensor_copy(
                self.ybB_t[:, dc * 128:(dc + 1) * 128], psB[:, 384:512]), after=[bcp],
                strict=[evr], tag=f"ybo{dc}")
            ybdone.append(evo)
            self.psB_chain = [evr, evo]
        self.tok_quant = [marks[(w, s)] for w in (0, 1) for s in range(32)] + [ybdone[-1]]
        if "quant" in self.taps:
            tw = self.tap("wfcq", [D, 4096], BF16)
            p.dma("sync", lambda e, tw=tw: e.dma_start(out=tw, in_=self.wfcq_h[:]),
                  after=self.tok_quant, tag="tapwf")
            tw2 = self.tap("wprq", [4096, D], BF16)
            p.dma("sync", lambda e, tw2=tw2: e.dma_start(out=tw2, in_=self.wprq_h[:]),
                  after=self.tok_quant, tag="tapwp")
            tw3 = self.tap("ybB", [128, 1024], F32)
            p.dma("sync", lambda e, tw3=tw3: e.dma_start(out=tw3, in_=self.ybB_t[:, 0:1024]),
                  after=self.tok_quant, tag="tapyb")
        yield

    def fc_gen(self):
        p, st1, sqsc, psA = self.p, self.st1, self.sqsc, self.psA
        tr_toks = self.tr_toks
        gsum, g2 = self.gst[:, 0:32], self.gst[:, 32:64]
        gt0, gt1 = self.gst[:, 70:71], self.gst[:, 71:72]
        hst_toks = {}
        wld_prev = {0: None, 1: None}
        gelu_prev = None
        gsq_prev = None
        gadd_prev = None
        mm_last = None
        for jc in range(8):
            bu = jc % 2
            if jc == 0:
                wld = self.wld0
            else:
                wld = p.dma("scalar", lambda e, jc=jc, bu=bu: e.dma_start(
                    out=self.wst_fc[bu].rearrange("p (dh j) -> p dh j", dh=8),
                    in_=self.wfcq_h.rearrange("(dh dl) j -> dl dh j", dl=128)
                        [:, :, jc * 512:(jc + 1) * 512]),
                    after=([wld_prev[bu]] if wld_prev[bu] is not None else
                           self.tok_quant + tr_toks[-1:]),
                    tag=f"Fw{jc}")
            for t in range(NT):
                pb = t % 2
                mms = []
                for dh in range(8):
                    dep = [wld, tr_toks[t // 8]]
                    if dh == 0:
                        if t >= 2:
                            dep.append(hst_toks.get((jc, t - 2, "gelu")))
                        elif jc > 0:
                            dep.append(hst_toks.get((jc - 1, NT - 2 + t, "gelu")))
                        if jc == 0 and t < 2:
                            dep += [self.tok_last_av]
                    mms.append(p.emit("tensor", lambda e, t=t, dh=dh, pb=pb, bu=bu: e.matmul(
                        psA[:, pb * 512:(pb + 1) * 512],
                        self.ln3T[:, (t * 8 + dh) * 128:(t * 8 + dh) * 128 + 128],
                        self.wst_fc[bu][:, dh * 512:(dh + 1) * 512],
                        start=(dh == 0), stop=(dh == 7)), after=dep, tag=f"F{jc}m{t}_{dh}"))
                mm_last = mms[-1]
                ge = p.emit("scalar", lambda e, t=t, pb=pb: e.activation(
                    self.hbuf[:, t * 512:(t + 1) * 512], psA[:, pb * 512:(pb + 1) * 512],
                    AF.Gelu, accum_out=gt0 if t % 2 == 0 else gt1),
                    after=[mms[-1], gadd_prev] + ([hst_toks[(jc - 1, "dma")]] if jc > 0 and t == 0
                                                  else []),
                    strict=[gelu_prev] if gelu_prev else [], tag=f"F{jc}g{t}")
                gelu_prev = ge
                hst_toks[(jc, t, "gelu")] = ge
                if jc == 0:
                    ga = p.emit("vector", lambda e, t=t: e.tensor_copy(
                        gsum[:, t:t + 1], gt0 if t % 2 == 0 else gt1), after=[ge],
                        strict=[gadd_prev] if gadd_prev else [], tag=f"F{jc}ga{t}")
                else:
                    ga = p.emit("vector", lambda e, t=t: e.tensor_add(
                        gsum[:, t:t + 1], gsum[:, t:t + 1], gt0 if t % 2 == 0 else gt1),
                        after=[ge], strict=[gadd_prev], tag=f"F{jc}ga{t}")
                gq = p.emit("scalar", lambda e, t=t: e.activation(
                    sqsc[:, (t % 2) * 512:(t % 2) * 512 + 512], self.hbuf[:, t * 512:(t + 1) * 512],
                    AF.Square, accum_out=gt0 if t % 2 == 0 else gt1),
                    after=[ge, ga], strict=[gsq_prev] if gsq_prev else [], tag=f"F{jc}q{t}")
                gsq_prev = gq
                if jc == 0:
                    gb = p.emit("vector", lambda e, t=t: e.tensor_copy(
                        g2[:, t:t + 1], gt0 if t % 2 == 0 else gt1), after=[gq], strict=[ga],
                        tag=f"F{jc}gb{t}")
                else:
                    gb = p.emit("vector", lambda e, t=t: e.tensor_add(
                        g2[:, t:t + 1], g2[:, t:t + 1], gt0 if t % 2 == 0 else gt1),
                        after=[gq], strict=[ga], tag=f"F{jc}gb{t}")
                gadd_prev = gb
                if t % 8 == 7:
                    yield
            hdma = p.dma("scalar", lambda e, jc=jc: e.dma_start(
                out=self.h_h.rearrange("(th tl) j -> tl th j", tl=128)[:, :, jc * 512:(jc + 1) * 512],
                in_=self.hbuf.rearrange("p (m l) -> p m l", l=512)),
                after=[hst_toks[(jc, NT - 1, "gelu")], gadd_prev], tag=f"Fh{jc}")
            hst_toks[(jc, "dma")] = hdma
            wld_prev[bu] = mm_last
        self.fc_state = (hst_toks, mm_last, gadd_prev, gsum, g2)
        yield

    def ffn_tail(self):
        p, st1, sqsc, psA = self.p, self.st1, self.sqsc, self.psA
        hst_toks, mm_last, gadd_prev, gsum, g2 = self.fc_state
        tr_toks = self.tr_toks
        finF = self.lnstats_finalize(gsum, g2, st1[:, 64:96], st1[:, 96:128], LN_EPS,
                                     [gadd_prev, gadd_prev], "Fst", n=4096)

        out_stores = []
        wld2_prev = {0: None, 1: None}
        af_prev = finF
        evy_prev = finF
        mmP_last = None
        mmP_prev_pc = None
        trh_prev = None
        ln4_sq_prev = None
        hld_prev = None
        ad_toks = {}
        ev_last = {}
        mm_slot = {}
        mm_slot_prev = {}
        ysb8 = self.ysb8

        def slot(tb8):
            return psA[:, tb8 * 512:tb8 * 512 + 256] if tb8 < 7 else self.psB[:, 0:256]

        for pc in range(4):
            trh = {}
            afs = {}
            for r in range(4):
                hlds = []
                for tbi in range(2):
                    tb8 = 2 * r + tbi
                    dep = [hld_prev] if hld_prev is not None else \
                        [hst_toks[(jj, "dma")] for jj in range(8)]
                    if r == 0 and trh_prev is not None:
                        dep += trh_prev
                    elif r > 0:
                        dep.append(trh[2 * r - 2])
                    hld = p.dma("gpsimd", lambda e, pc=pc, r=r, tbi=tbi: e.dma_start(
                        out=self.hbuf[:, tbi * 4096:(tbi + 1) * 4096],
                        in_=self.h_h.rearrange("(tt tl) j -> tt tl j", tl=128)[8 * pc + 2 * r + tbi]),
                        after=dep, tag=f"Ph{pc}_{r}_{tbi}")
                    hlds.append(hld)
                    hld_prev = hld
                    t = pc * 8 + tb8
                    afs[tb8] = p.emit("vector", lambda e, t=t, tbi=tbi: e.tensor_scalar(
                        self.hbuf[:, tbi * 4096:(tbi + 1) * 4096],
                        self.hbuf[:, tbi * 4096:(tbi + 1) * 4096],
                        st1[:, 64 + t:65 + t], st1[:, 96 + t:97 + t], ALU.mult, ALU.add),
                        after=[hld, finF], strict=[af_prev], tag=f"Pa{pc}_{tb8}")
                    af_prev = afs[tb8]
                trdep = [afs[2 * r], afs[2 * r + 1]]
                if pc == 0:
                    trdep.append(mm_last)
                else:
                    trdep += [mm_slot_prev[2 * r], mm_slot_prev[2 * r + 1]]
                tr = p.dma("sync", lambda e, r=r: e.dma_start(
                    out=self.pool[:, r * 8192:(r + 1) * 8192].rearrange("p (m l) -> p m l", l=128),
                    in_=self.hbuf[:, 0:8192], transpose=True),
                    after=trdep, xpose=True, tag=f"Pt{pc}_{r}")
                trh[2 * r] = trh[2 * r + 1] = tr
            evs = {}
            for dc in range(4):
                wbu = dc % 2
                wdep = ([wld2_prev[wbu]] if wld2_prev[wbu] is not None else
                        self.tok_quant + (tr_toks[-1:] if wbu == 0 else [mm_last]))
                wld2a = p.dma("scalar", lambda e, dc=dc, wbu=wbu: e.dma_start(
                    out=self.wst_pr[wbu].rearrange("p (jh d) -> p jh d", jh=32)[:, 0:16],
                    in_=self.wprq_h.rearrange("(jh jl) d -> jl jh d", jl=128)
                        [:, 0:16, dc * 256:(dc + 1) * 256]), after=wdep, tag=f"Pw{pc}_{dc}a")
                wld2b = p.dma("gpsimd", lambda e, dc=dc, wbu=wbu: e.dma_start(
                    out=self.wst_pr[wbu].rearrange("p (jh d) -> p jh d", jh=32)[:, 16:32],
                    in_=self.wprq_h.rearrange("(jh jl) d -> jl jh d", jl=128)
                        [:, 16:32, dc * 256:(dc + 1) * 256]), after=wdep, tag=f"Pw{pc}_{dc}b")
                for tb8 in range(8):
                    mms = []
                    for jh in range(32):
                        dep = [wld2a if jh < 16 else wld2b, trh[tb8]]
                        if jh == 0:
                            prev_ev = ev_last.get(tb8)
                            if prev_ev is not None:
                                dep.append(prev_ev)
                            if pc == 0 and dc == 0:
                                dep += self.tok_quant
                                if tb8 < 2:
                                    dep += [hst_toks[(7, NT - 1, "gelu")],
                                            hst_toks[(7, NT - 2, "gelu")]]
                                if tb8 == 7:
                                    dep += self.psB_chain
                        mms.append(p.emit("tensor", lambda e, jh=jh, tb8=tb8, dc=dc,
                                          wbu=wbu: e.matmul(
                            slot(tb8),
                            self.hT8[tb8][:, jh * 128:(jh + 1) * 128],
                            self.wst_pr[wbu][:, jh * 256:(jh + 1) * 256],
                            start=(jh == 0), stop=(jh == 31)), after=dep,
                            tag=f"P{pc}m{dc}_{tb8}_{jh}"))
                    mmP_last = mms[-1]
                    mm_slot[tb8] = mms[-1]
                    evdep = [mms[-1]]
                    if pc == 0 and dc == 0 and tb8 == 0:
                        evdep += [hst_toks[(7, "dma")], finF]
                    ev = p.emit("vector", lambda e, tb8=tb8, dc=dc: e.tensor_add(
                        ysb8[:, tb8 * 1024 + dc * 256:tb8 * 1024 + (dc + 1) * 256],
                        slot(tb8), self.ybB_t[:, dc * 256:(dc + 1) * 256]),
                        after=evdep, strict=[evy_prev], tag=f"Pe{pc}_{dc}_{tb8}")
                    evy_prev = ev
                    evs[(dc, tb8)] = ev
                    ev_last[tb8] = ev
                wld2_prev[wbu] = mmP_last
            mmP_prev_pc = mmP_last
            mm_slot_prev = dict(mm_slot)
            trh_prev = [trh[0], trh[2], trh[4], trh[6]]
            s8, q8 = self.gst[:, 0:8], self.gst[:, 8:16]
            mu8, v8, t8 = self.gst[:, 16:24], self.gst[:, 24:32], self.gst[:, 32:40]
            rstd8, nmu8 = self.gst[:, 40:48], self.gst[:, 48:56]
            sstats = []
            for tb8 in range(8):
                sm = p.emit("vector", lambda e, tb8=tb8: e.reduce_sum(
                    s8[:, tb8:tb8 + 1], ysb8[:, tb8 * 1024:(tb8 + 1) * 1024], axis=AX.X),
                    after=[evs[(3, tb8)]], strict=[evs[(3, tb8)]], tag=f"P4s{pc}_{tb8}")
                sq = p.emit("scalar", lambda e, tb8=tb8: e.activation(
                    sqsc[:, 1024:2048], ysb8[:, tb8 * 1024:(tb8 + 1) * 1024], AF.Square,
                    accum_out=q8[:, tb8:tb8 + 1]),
                    after=[evs[(3, tb8)]], strict=[ln4_sq_prev] if ln4_sq_prev else [],
                    tag=f"P4q{pc}_{tb8}")
                ln4_sq_prev = sq
                sstats += [sm, sq]
            m4 = p.emit("vector", lambda e: e.tensor_scalar(mu8, s8, 1.0 / D, None,
                                                            ALU.mult), after=sstats,
                        strict=sstats[-2:], tag=f"P4m{pc}")
            q4 = p.emit("vector", lambda e: e.tensor_mul(v8, mu8, mu8), strict=[m4], tag=f"P4v{pc}")
            q5 = p.emit("vector", lambda e: e.tensor_scalar(t8, q8, 1.0 / D, LN_EPS,
                                                            ALU.mult, ALU.add),
                        strict=[q4], tag=f"P4w{pc}")
            q6 = p.emit("vector", lambda e: e.tensor_sub(v8, t8, v8), strict=[q5], tag=f"P4u{pc}")
            nr4 = self.newton_rsqrt(rstd8, v8, 8, [q6], tagp=f"P4{pc}")
            nm4 = p.emit("vector", lambda e: e.tensor_mul(nmu8, mu8, rstd8),
                         strict=[nr4], tag=f"P4n{pc}")
            nm4b = p.emit("vector", lambda e: e.tensor_scalar(nmu8, nmu8, -1.0, None, ALU.mult),
                          strict=[nm4], tag=f"P4nb{pc}")
            for tb8 in range(8):
                t = pc * 8 + tb8
                bu = t % 2
                lx = p.dma("sync", lambda e, t=t, bu=bu: e.dma_start(
                    out=self.x1s2_f[bu], in_=self.x1_h[t * 128:(t + 1) * 128, :]),
                    after=[self.x1_store[t]] + ([out_stores[-2]] if len(out_stores) >= 2
                                                else [nm4b, mmP_last]),
                    tag=f"Px{t}")
                ln4 = p.emit("vector", lambda e, tb8=tb8, bu=bu: e.tensor_scalar(
                    sqsc[:, bu * 1024:(bu + 1) * 1024], ysb8[:, tb8 * 1024:(tb8 + 1) * 1024],
                    rstd8[:, tb8:tb8 + 1], nmu8[:, tb8:tb8 + 1], ALU.mult, ALU.add),
                    after=[nm4b], strict=[evy_prev], tag=f"Pl{t}")
                evy_prev = ln4
                ad = p.emit("vector", lambda e, bu=bu: e.tensor_add(
                    self.outstP[:, bu * 1024:(bu + 1) * 1024], sqsc[:, bu * 1024:(bu + 1) * 1024],
                    self.x1s2_f[bu]),
                    after=[lx], strict=[ln4], tag=f"Po{t}")
                ad_toks[t] = ad
                st_ = p.dma("sync", lambda e, t=t, bu=bu: e.dma_start(
                    out=self.out_h[t * 128:(t + 1) * 128, :],
                    in_=self.outstP[:, bu * 1024:(bu + 1) * 1024]), after=[ad], tag=f"Ps{t}")
                out_stores.append(st_)
        self.out_stores = out_stores

    def build(self, stage="full"):
        self.hstate = {}
        self.psB_chain = []
        self.ln1T_tok = [None, None]
        self.ln1T_users = [[], []]
        self.ln1c_users = [[], []]
        self.chunk_half_done = {}
        self.ust_toks = []
        self.den_toks = []
        self.phase0()
        self.quant_prep()
        self.hstate["ctx_init"] = [self.tok_lncols]
        qgen = self.quant_units()
        self.phase1()

        def pump(n):
            for _ in range(n):
                try:
                    next(qgen)
                except StopIteration:
                    break
        self.phase1_pump = pump

        def drain(g):
            for _ in g:
                pass
        self.chunk(0)
        drain(self.prep_gen(0))
        for h in range(H):
            qg = self.qc_gen(h)
            pg = None
            if h + 1 < H:
                if (h + 1) % 2 == 0:
                    self.chunk((h + 1) // 2)
                pg = self.prep_gen(h + 1)
            nq = 0
            for _ in qg:
                nq += 1
                for _ in range(2 if nq <= 20 else 1):
                    if pg is None:
                        break
                    try:
                        next(pg)
                    except StopIteration:
                        pg = None
                if nq % 36 == 0:
                    pump(1)
            if pg is not None:
                drain(pg)
            pump(2)
        pump(1000)
        if stage == "attn":
            self.p.emit("sync", lambda e: e.nop(), after=self.ust_toks + self.den_toks, tag="fin")
            self.p.build().close()
            return self.nc, self.dbg
        p4g = self.phase4_gen()
        if stage in ("x1", "quant"):
            drain(p4g)
            fin = self.x1_store if stage == "x1" else self.tok_quant + self.x1_store
            self.p.emit("sync", lambda e: e.nop(), after=fin, tag="fin")
            self.p.build().close()
            return self.nc, self.dbg
        next(p4g)
        for _ in range(3):
            next(p4g)
        fcg = self.fc_gen()
        for g in range(1, 4):
            for _ in range(3):
                next(p4g)
            next(fcg)
        drain(p4g)
        drain(fcg)
        self.ffn_tail()
        self.p.emit("sync", lambda e: e.nop(), after=self.out_stores, tag="fin")
        self.p.build().close()
        return self.nc, self.dbg


def build_nc(stage=None, taps=None):
    mk = MK(taps if taps is not None else DEBUG_TAPS)
    return mk.build(stage if stage is not None else DEBUG_STAGE)


def kernel(L0_attn_x, L0_attn_W_DQ, L0_attn_W_UQ, L0_attn_W_DKV, L0_attn_W_UKV,
           L0_attn_W_QR, L0_attn_W_KR, L0_attn_W_O, L0_attn_q_norm_w,
           L0_attn_k_norm_w, L0_ffn_W_fc, L0_ffn_ln_w, L0_ffn_ln_b, L0_ffn_W_proj):
    x = np.ascontiguousarray(np.asarray(L0_attn_x, dtype=np.float32))
    nc, _ = build_nc()
    common = {
        "W_DQ": np.asarray(L0_attn_W_DQ, np.float32), "W_UQ": np.asarray(L0_attn_W_UQ, np.float32),
        "W_DKV": np.asarray(L0_attn_W_DKV, np.float32),
        "W_UKV": np.asarray(L0_attn_W_UKV, np.float32),
        "W_QR": np.asarray(L0_attn_W_QR, np.float32), "W_KR": np.asarray(L0_attn_W_KR, np.float32),
        "W_O": np.asarray(L0_attn_W_O, np.float32),
        "q_norm_w": np.asarray(L0_attn_q_norm_w, np.float32),
        "k_norm_w": np.asarray(L0_attn_k_norm_w, np.float32),
        "W_fc": np.asarray(L0_ffn_W_fc, np.float32), "ln_w": np.asarray(L0_ffn_ln_w, np.float32),
        "ln_b": np.asarray(L0_ffn_ln_b, np.float32), "W_proj": np.asarray(L0_ffn_W_proj, np.float32),
    }
    in_maps = [dict(common, x=np.ascontiguousarray(x[b])) for b in range(8)]
    res = run_bass_kernel_spmd(nc, in_maps, core_ids=list(range(8)))
    out = np.stack([res.results[b]["out"] for b in range(8)], axis=0)
    return out.astype(np.float32)


# version 34
# speedup vs baseline: 1.0280x; 1.0280x over previous
import contextlib
import numpy as np
import concourse.bass as bass
import concourse.mybir as mybir
from concourse.bass_utils import run_bass_kernel_spmd

F32 = mybir.dt.float32
BF16 = mybir.dt.bfloat16
I32 = mybir.dt.int32
AF = mybir.ActivationFunctionType
ALU = mybir.AluOpType
AX = mybir.AxisListType

T = 4096
D = 1024
H = 16
NT = 32
NCH = 8
RMS_EPS = 1.1920929e-07
LN_EPS = 1e-5
ABS_EPS = 1e-6

DEBUG_STAGE = "full"
DEBUG_TAPS = ()

ENGINES = ("sync", "scalar", "vector", "tensor", "gpsimd")


class _Op:
    __slots__ = ("eng", "fn", "after", "is_dma", "needs_inc", "inc_val",
                 "tag", "strict", "fence", "xpose", "lane", "fence_deps")

    def __init__(self, eng, fn, after, is_dma, tag):
        self.eng, self.fn, self.after, self.is_dma, self.tag = eng, fn, after, is_dma, tag
        self.needs_inc = False
        self.inc_val = 0
        self.strict = ()
        self.fence = False
        self.xpose = False


class Prog:
    def __init__(self, nc):
        self.nc = nc
        self.ops = {e: [] for e in ENGINES}
        self.all_ops = []

    def emit(self, eng, fn, after=(), strict=(), tag=""):
        op = _Op(eng, fn, tuple(t for t in after if t is not None), False, tag)
        op.strict = tuple(t for t in strict if t is not None)
        self.ops[eng].append(op)
        self.all_ops.append(op)
        return op

    def dma(self, eng, fn, after=(), fence=False, xpose=False, tag=""):
        op = _Op(eng, fn, tuple(t for t in after if t is not None), True, tag)
        op.fence = fence or xpose
        op.xpose = xpose
        self.ops[eng].append(op)
        self.all_ops.append(op)
        return op

    def build(self):
        nc = self.nc
        NLANE = 8
        for op in self.all_ops:
            if op.is_dma:
                op.needs_inc = True
            for dep in op.strict:
                dep.needs_inc = True
            for dep in op.after:
                if dep.eng != op.eng or (dep.is_dma and not op.is_dma):
                    dep.needs_inc = True
        cum = {e: 0 for e in ENGINES}
        lane_cum = {}
        dma_idx = {e: 0 for e in ENGINES}
        all_lanes = {}
        prev_dmas = {e: [] for e in ENGINES}
        for op in self.all_ops:
            if op.is_dma:
                lane = dma_idx[op.eng] % NLANE
                op.lane = (op.eng, lane)
                dma_idx[op.eng] += 1
                if op.fence:
                    op.fence_deps = tuple(prev_dmas[op.eng])
                lane_cum[op.lane] = lane_cum.get(op.lane, 0) + 16
                op.inc_val = lane_cum[op.lane]
                lst = [d for d in prev_dmas[op.eng] if d.lane != op.lane]
                prev_dmas[op.eng] = lst + [op]
            else:
                if op.needs_inc:
                    cum[op.eng] += 1
                op.inc_val = cum[op.eng]
        self.sems = {}
        stack = contextlib.ExitStack()
        for e in ENGINES:
            if any((not op.is_dma) and op.needs_inc for op in self.ops[e]):
                self.sems[e] = stack.enter_context(nc.semaphore(f"sem_{e}"))
        for key in lane_cum:
            self.sems[key] = stack.enter_context(nc.semaphore(f"dsem_{key[0]}_{key[1]}"))
        block = stack.enter_context(nc.Block())
        for e in ENGINES:
            if not self.ops[e]:
                continue
            sec = getattr(block, e)

            def emit_stream(engine, ops_e=self.ops[e], e=e):
                waited = {}

                def wait(dep):
                    key = dep.lane if dep.is_dma else dep.eng
                    v = dep.inc_val
                    if waited.get(key, -1) < v:
                        engine.wait_ge(self.sems[key], v)
                        waited[key] = v

                for op in ops_e:
                    if op.fence:
                        for dep in getattr(op, "fence_deps", ()):
                            wait(dep)
                    for dep in op.strict:
                        wait(dep)
                    for dep in op.after:
                        if dep.eng == e and not dep.is_dma and not op.is_dma:
                            continue
                        wait(dep)
                    ins = op.fn(engine)
                    if op.is_dma:
                        ins.then_inc(self.sems[op.lane], 16)
                    elif op.needs_inc:
                        ins.then_inc(self.sems[e], 1)

            sec(emit_stream)
        return stack


class MK:
    def __init__(self, taps=()):
        self.taps = set(taps)
        self.nc = nc = bass.Bass("TRN2", target_bir_lowering=False)
        self.p = Prog(nc)
        self.dbg = {}

        dr = nc.dram_tensor
        self.x_h = dr("x", [T, D], F32, kind="ExternalInput").ap()
        for nm, sh in (("W_DQ", [512, 64]), ("W_UQ", [64, 512]), ("W_DKV", [1024, 64]),
                       ("W_UKV", [128, 1024]), ("W_QR", [64, 64]), ("W_KR", [64, 64]),
                       ("W_O", [64, 64]), ("q_norm_w", [64]), ("k_norm_w", [64]),
                       ("W_fc", [4096, 1024]), ("ln_w", [4096]), ("ln_b", [4096]),
                       ("W_proj", [1024, 4096])):
            setattr(self, nm, dr(nm, sh, F32, kind="ExternalInput").ap())
        self.out_h = dr("out", [T, D], F32, kind="ExternalOutput").ap()
        self.ut_h = dr("ut_scr", [D, T], BF16, kind="Internal").ap()
        self.x1_h = dr("x1_scr", [T, D], F32, kind="Internal").ap()
        self.wfcq_h = dr("wfcq_scr", [D, 4096], BF16, kind="Internal").ap()
        self.wprq_h = dr("wprq_scr", [4096, D], BF16, kind="Internal").ap()
        self.h_h = dr("h_scr", [T, 4096], BF16, kind="Internal").ap()

        A = lambda nm, sh, dt: nc.alloc_sbuf_tensor(nm, sh, dt).ap()
        self.identf = A("identf", [128, 128], F32)
        self.onesb = A("onesb", [1, 128], BF16)
        self.onesf = A("onesf", [1, 128], F32)
        self.wpa = A("wpa", [128, 384], BF16)
        self.lnw_c = A("lnw_c", [128, 32], F32)
        self.lnb_c = A("lnb_c", [128, 32], F32)
        self.lnb_b = A("lnb_b", [128, 32], BF16)
        self.lnbB = A("lnbB", [128, 4096], BF16)
        self.ybB_t = A("ybB_t", [128, 1040], F32)
        self.st1 = A("st1", [128, 256], F32)
        self.stp = A("stp", [128, 192], F32)
        self.stx = A("stx", [128, 96], F32)
        self.nsc = A("nsc", [128, 64], F32)
        self.nsci = A("nsci", [128, 64], I32)
        self.biasM = A("biasM", [128, 2], F32)
        self.sqsc = A("sqsc", [128, 2560], F32)
        self.rows = A("rows", [1, 512], F32)
        self.gst = A("gst", [128, 96], F32)

        self.pool = A("pool", [128, 32768], BF16)
        self.pool2 = A("pool2", [128, 12288], F32)
        self.pool3 = A("pool3", [128, 24576], BF16)

        P, P2, P3 = self.pool, self.pool2, self.pool3
        P2b = P2.bitcast(BF16)
        Pf = P.bitcast(F32)
        self.wb = P[:, 0:2368]
        self.q_st = P[:, 0:4096]
        self.k_st = P[:, 4096:8192]
        self.qT = P[:, 8192:12288]
        self.kT = P[:, 12288:16384]
        self.ln1c = [P[:, 16384:20480], P[:, 20480:24576]]
        self.ln1T = [P[:, 24576:28672], P[:, 28672:32768]]
        self.a_tok = P[:, 0:32768]
        self.ln3T = P[:, 0:32768]
        self.hT8 = [P[:, i * 4096:(i + 1) * 4096] for i in range(8)]
        self.outstP = P2[:, 8192:10240]
        self.x1s2_f = [P2[:, 10240:11264], P2[:, 11264:12288]]

        self.wsc = P2[:, 0:1536]
        self.wsc2 = P2[:, 1536:3072]
        self.x_stage = P2[:, 0:4096]
        self.x1slots = [P2[:, 4096:5120], P2[:, 7168:8192], P2[:, 10240:11264],
                        P2[:, 11264:12288], P2[:, 2048:3072], P2[:, 3072:4096],
                        P2[:, 8192:9216], P2[:, 9216:10240]]
        self.den16 = P2b[0:16, 20480:24576]
        self.lnw_row = P2[0:1, 4096:8192]
        self.qw_f32 = [P2[:, 4096:5120], P2[:, 5120:6144]]
        self.qw_ind = P2[:, 6144:7168]
        self.qw_sq = P2[:, 7168:8192]
        self.qw_sgn = P2b[:, 16384:17408]
        self.qw_q = P2b[:, 17408:18432]
        self.qw_tr = [P2b[:, 18432:19456], P2b[:, 19456:20480]]

        self.wst_fc = [P2b[:, 10240:14336], P2b[:, 14336:18432]]
        self.wst_pr = [P2b[:, 0:8192], P2b[:, 8192:16384]]

        self.cmask = A("cmask", [128, 2048], BF16)
        self.qT2 = A("qT2", [128, 4096], BF16)
        self.v_st2 = [P3[:, 0:2080], P3[:, 18944:21024]]
        self.v_st = P3[:, 0:2080]
        self.ebuf = P3[:, 3072:6144]
        self.uh = P3[:, 6144:10240]
        self.kT2 = P3[:, 10240:14336]
        self.qT_v = [self.qT, self.qT2]
        self.kT_v = [self.kT, self.kT2]
        self.denst = P3[64:65, 14336:18432]
        self.denT = P3[:, 18432:18944]
        self.hbuf = P3[:, 0:16384]
        self.ln3tok = P3[:, 16384:24576]
        self.ysb8 = P3.bitcast(F32)[:, 4096:12288]

        self.psA = nc.alloc_psum_tensor("psA", [128, 3584], F32).ap()
        self.psB = nc.alloc_psum_tensor("psB", [128, 512], F32).ap()
        self.sb_left = nc.sbuf_bytes_remaining

    def tap(self, name, shape, dt=F32):
        ap = self.nc.dram_tensor("dbg_" + name, shape, dt, kind="ExternalOutput").ap()
        self.dbg[name] = ap
        return ap

    def newton_rsqrt(self, dst, src, n, after, iters=3, tagp=""):
        p = self.p
        np_ = src.shape[0]
        i0, i1 = self.nsci[0:np_, 0:n], self.nsci[0:np_, 32:32 + n]
        tt = self.nsc[0:np_, 0:n]
        c1 = p.emit("vector", lambda e: e.tensor_copy(i0, src.bitcast(I32)), after=after,
                    strict=after, tag=tagp + "Nc")
        sh = p.emit("vector", lambda e: e.tensor_scalar(i1, i0, 1, None, ALU.arith_shift_right),
                    strict=[c1], tag=tagp + "Ns")
        mg = p.emit("vector", lambda e: e.tensor_scalar(i0, i1, -1.0, 1597463007.0, ALU.mult, ALU.add),
                    strict=[sh], tag=tagp + "Nm")
        prev = p.emit("vector", lambda e: e.tensor_copy(dst, i0.bitcast(F32)), strict=[mg],
                      tag=tagp + "Nb")
        for it in range(iters):
            a1 = p.emit("vector", lambda e: e.tensor_mul(tt, src, dst), strict=[prev], tag=tagp + f"N{it}a")
            a2 = p.emit("vector", lambda e: e.tensor_mul(tt, tt, dst), strict=[a1], tag=tagp + f"N{it}b")
            a3 = p.emit("vector", lambda e: e.tensor_scalar(tt, tt, -0.5, 1.5, ALU.mult, ALU.add),
                        strict=[a2], tag=tagp + f"N{it}c")
            prev = p.emit("vector", lambda e: e.tensor_mul(dst, dst, tt), strict=[a3], tag=tagp + f"N{it}d")
        return prev

    def lnstats_finalize(self, sumcol, sqcol, rstd, negmur, eps, after, tagp, n=D):
        p, st1 = self.p, self.st1
        w = sumcol.shape[1]
        mu = self.stx[:, 0:w]
        v_ = self.stx[:, 32:32 + w]
        m1 = p.emit("vector", lambda e: e.tensor_scalar(mu, sumcol, 1.0 / n, None, ALU.mult),
                    after=after, strict=after[-2:], tag=tagp + "mu")
        v1 = p.emit("vector", lambda e: e.tensor_mul(v_, mu, mu), strict=[m1], tag=tagp + "m2")
        v2 = p.emit("vector", lambda e: e.tensor_scalar(self.nsc[:, 32:32 + w], sqcol, 1.0 / n, eps,
                                                        ALU.mult, ALU.add), strict=[v1], tag=tagp + "e2")
        v3 = p.emit("vector", lambda e: e.tensor_sub(v_, self.nsc[:, 32:32 + w], v_), strict=[v2],
                    tag=tagp + "v")
        nr = self.newton_rsqrt(rstd, v_, w, [v3], tagp=tagp)
        nm = p.emit("vector", lambda e: e.tensor_mul(negmur, mu, rstd), strict=[nr], tag=tagp + "nm")
        return p.emit("vector", lambda e: e.tensor_scalar(negmur, negmur, -1.0, None, ALU.mult),
                      strict=[nm], tag=tagp + "nm2")

    def phase0(self):
        p = self.p
        t0 = p.emit("gpsimd", lambda e: e.memset(self.identf[:], 0.0), tag="id0")
        self.tok_id = p.emit("gpsimd", lambda e: e.affine_select(
            out=self.identf[:], in_=self.identf[:], compare_op=ALU.not_equal, fill=1.0,
            base=0, pattern=[[-1, 128]], channel_multiplier=1), strict=[t0], tag="id1")
        self.tok_ones = p.emit("gpsimd", lambda e: e.memset(self.onesb[0:1, :], 1.0), tag="onesb")
        self.tok_onesf = p.emit("gpsimd", lambda e: e.memset(self.onesf[0:1, :], 1.0), tag="onesf")
        self.tok_vone = p.emit("gpsimd", lambda e: e.memset(
            self.v_st.rearrange("p (m l) -> p m l", l=65)[:, :, 64:65], 1.0), tag="vone")
        tok_vone2 = p.emit("gpsimd", lambda e: e.memset(
            self.v_st2[1].rearrange("p (m l) -> p m l", l=65)[:, :, 64:65], 1.0),
            strict=[self.tok_vone], tag="vone2")
        self.tok_vone_v = [self.tok_vone, tok_vone2]
        cm0 = p.emit("gpsimd", lambda e: e.memset(self.cmask[:], 1.0), tag="cm0")
        cmts = [cm0]
        for rr in range(4):
            cmts.append(p.emit("gpsimd", lambda e, rr=rr: e.affine_select(
                out=self.cmask[:, rr * 512:(rr + 1) * 512],
                in_=self.cmask[:, rr * 512:(rr + 1) * 512],
                compare_op=ALU.is_ge, fill=0.0,
                base=-128 * rr, pattern=[[1, 512]], channel_multiplier=-1),
                strict=[cmts[-1]], tag=f"cm{rr}"))
        self.tok_cmask = cmts[-1]

        wsc, wsc2, wb, psA = self.wsc, self.wsc2, self.wb, self.psA
        dq = [p.dma("sync", lambda e, c=c: e.dma_start(
            out=wsc[:, c * 64:(c + 1) * 64], in_=self.W_DQ[c * 128:(c + 1) * 128, :]), tag=f"ldq{c}")
            for c in range(4)]
        uq = p.dma("sync", lambda e: e.dma_start(out=wsc[0:64, 256:768], in_=self.W_UQ[:]), tag="lduq")
        dkv = [p.dma("sync", lambda e, c=c: e.dma_start(
            out=wsc[:, 768 + c * 64:768 + (c + 1) * 64], in_=self.W_DKV[c * 128:(c + 1) * 128, :]),
            tag=f"ldkv{c}") for c in range(8)]
        ukv = p.dma("sync", lambda e: e.dma_start(out=wsc2[:, 0:1024], in_=self.W_UKV[:]), tag="ldukv")
        qr = p.dma("sync", lambda e: e.dma_start(out=wsc2[0:64, 1024:1088], in_=self.W_QR[:]), tag="ldqr")
        kr = p.dma("sync", lambda e: e.dma_start(out=wsc2[0:64, 1088:1152], in_=self.W_KR[:]), tag="ldkr")
        wo = p.dma("sync", lambda e: e.dma_start(out=wsc2[0:64, 1152:1216], in_=self.W_O[:]), tag="ldwo")
        qnw = p.dma("sync", lambda e: e.dma_start(
            out=wsc2[0:64, 1216:1217], in_=self.q_norm_w.rearrange("(a b) -> a b", b=1)), tag="ldqnw")
        knw = p.dma("sync", lambda e: e.dma_start(
            out=wsc2[0:64, 1217:1218], in_=self.k_norm_w.rearrange("(a b) -> a b", b=1)), tag="ldknw")

        c_dq = p.emit("vector", lambda e: e.tensor_copy(wb[:, 0:256], wsc[:, 0:256]), after=dq, tag="cdq")
        c_dkv = p.emit("vector", lambda e: e.tensor_copy(wb[:, 512:1024], wsc[:, 768:1280]),
                       after=dkv, tag="cdkv")
        tuq = [p.emit("tensor", lambda e, c=c: e.transpose(
            psA[:, c * 512:c * 512 + 64], wsc[0:64, 256 + c * 128:256 + (c + 1) * 128], self.identf[0:64, 0:64]),
            after=[uq, self.tok_id], tag=f"tuq{c}") for c in range(4)]
        e_uq = p.emit("vector", lambda e: e.tensor_copy(
            wb[:, 256:512].rearrange("p (c n) -> p c n", c=4),
            psA[:, 0:2048].rearrange("p (c n) -> p c n", c=4)[:, :, 0:64]),
            after=tuq, strict=[c_dkv], tag="euq")
        e_ukv = []
        for rnd in range(2):
            tks = [p.emit("tensor", lambda e, rnd=rnd, c=c: e.transpose(
                psA[:, c * 512:c * 512 + 128],
                wsc2[:, rnd * 512 + c * 128:rnd * 512 + (c + 1) * 128], self.identf[:]),
                after=[ukv, self.tok_id, e_uq if rnd == 0 else e_ukv[0]], tag=f"tukv{rnd}{c}")
                for c in range(4)]
            e_ukv.append(p.emit("vector", lambda e, rnd=rnd: e.tensor_copy(
                wb[:, 1024 + rnd * 512:1536 + rnd * 512].rearrange("p (c n) -> p c n", c=4),
                psA[:, 0:2048].rearrange("p (c n) -> p c n", c=4)[:, :, 0:128]),
                after=tks, strict=[e_uq], tag=f"eukv{rnd}"))
        tqr = p.emit("tensor", lambda e: e.transpose(psA[0:64, 0:64], wsc2[0:64, 1024:1088],
                                                     self.identf[0:64, 0:64]), after=[qr, e_ukv[1]], tag="tqr")
        tkr = p.emit("tensor", lambda e: e.transpose(psA[0:64, 512:576], wsc2[0:64, 1088:1152],
                                                     self.identf[0:64, 0:64]), after=[kr], tag="tkr")
        two = p.emit("tensor", lambda e: e.transpose(psA[0:64, 1024:1088], wsc2[0:64, 1152:1216],
                                                     self.identf[0:64, 0:64]), after=[wo], tag="two")
        e_qr = p.emit("vector", lambda e: e.tensor_copy(wb[0:64, 2048:2112], psA[0:64, 0:64]),
                      after=[tqr], strict=[e_ukv[1]], tag="eqr")
        e_kr = p.emit("vector", lambda e: e.tensor_copy(wb[0:64, 2112:2176], psA[0:64, 512:576]),
                      after=[tkr], tag="ekr")
        e_wo = p.emit("vector", lambda e: e.tensor_copy(wb[0:64, 2176:2240], psA[0:64, 1024:1088]),
                      after=[two], tag="ewo")
        mmAq = None
        for c in range(4):
            mmAq = p.emit("tensor", lambda e, c=c: e.matmul(
                psA[0:64, 2560:2624], wb[:, c * 64:(c + 1) * 64], wb[:, 256 + c * 64:256 + (c + 1) * 64],
                start=(c == 0), stop=(c == 3)), after=[c_dq, e_uq, two, e_qr, e_kr, e_wo],
                tag=f"mAq{c}")
        e_Aq = p.emit("vector", lambda e: e.tensor_copy(wsc[0:64, 1280:1344], psA[0:64, 2560:2624]),
                      after=[mmAq], strict=[e_wo], tag="eAq")
        mmAqT = None
        for c in range(4):
            mmAqT = p.emit("tensor", lambda e, c=c: e.matmul(
                psA[0:64, 3072:3136], wb[:, 256 + c * 64:256 + (c + 1) * 64], wb[:, c * 64:(c + 1) * 64],
                start=(c == 0), stop=(c == 3)), after=[mmAq], tag=f"mAqT{c}")
        e_AqT = p.emit("vector", lambda e: e.tensor_copy(wsc[0:64, 1344:1408], psA[0:64, 3072:3136]),
                       after=[mmAqT], strict=[e_Aq], tag="eAqT")
        mmAkv = None
        for c in range(8):
            mmAkv = p.emit("tensor", lambda e, c=c: e.matmul(
                psA[0:64, 2560:2688], wb[:, 512 + c * 64:512 + (c + 1) * 64],
                wb[:, 1024 + c * 128:1024 + (c + 1) * 128],
                start=(c == 0), stop=(c == 7)), after=[c_dkv, e_ukv[1], e_Aq], tag=f"mAkv{c}")
        e_Akv = p.emit("vector", lambda e: e.tensor_copy(wsc2[0:64, 1280:1408], psA[0:64, 2560:2688]),
                       after=[mmAkv], strict=[e_AqT], tag="eAkv")
        wqk = p.emit("vector", lambda e: e.tensor_mul(wsc2[0:64, 1218:1219], wsc2[0:64, 1216:1217],
                                                      wsc2[0:64, 1217:1218]),
                     after=[qnw, knw], strict=[e_Akv], tag="wqk")
        aqwT = p.emit("vector", lambda e: e.tensor_scalar(wsc[0:64, 1408:1472], wsc[0:64, 1344:1408],
                                                          wsc2[0:64, 1216:1217], None, ALU.mult),
                      after=[qnw], strict=[wqk], tag="aqwT")
        aq2T = p.emit("vector", lambda e: e.tensor_scalar(wsc[0:64, 1472:1536], wsc[0:64, 1344:1408],
                                                          wsc2[0:64, 1218:1219], None, ALU.mult),
                      strict=[aqwT], tag="aq2T")
        c_aqwT = p.emit("vector", lambda e: e.tensor_copy(wb[0:64, 2240:2304], wsc[0:64, 1408:1472]),
                        strict=[aq2T], tag="caqwT")
        tAq2 = p.emit("tensor", lambda e: e.transpose(psA[0:64, 3072:3136], wsc[0:64, 1472:1536],
                                                      self.identf[0:64, 0:64]), after=[aq2T, e_AqT, mmAkv],
                      tag="tAq2")
        aAq2 = p.emit("vector", lambda e: e.tensor_copy(self.wpa[0:64, 192:256], psA[0:64, 3072:3136]),
                      after=[tAq2], strict=[c_aqwT], tag="aAq2")
        tAvT = p.emit("tensor", lambda e: e.transpose(psA[0:64, 2560:2624], wsc2[0:64, 1344:1408],
                                                      self.identf[0:64, 0:64]), after=[e_Akv, tAq2], tag="tAvT")
        e_AvT = p.emit("vector", lambda e: e.tensor_copy(wb[0:64, 2304:2368], psA[0:64, 2560:2624]),
                       after=[tAvT], strict=[aAq2], tag="eAvT")
        mmAqr = p.emit("tensor", lambda e: e.matmul(psA[0:64, 3072:3136], wb[0:64, 2240:2304],
                                                    wb[0:64, 2048:2112], start=True, stop=True),
                       after=[c_aqwT, e_qr, aAq2, tAvT], tag="mAqr")
        aAqr = p.emit("vector", lambda e: e.tensor_copy(self.wpa[0:64, 256:320], psA[0:64, 3072:3136]),
                      after=[mmAqr], strict=[e_AvT], tag="aAqr")
        mmAvv = p.emit("tensor", lambda e: e.matmul(psA[0:64, 2560:2624], wb[0:64, 2304:2368],
                                                    wb[0:64, 2176:2240], start=True, stop=True),
                       after=[e_AvT, e_wo, mmAqr], tag="mAvv")
        aAvv = p.emit("vector", lambda e: e.tensor_copy(self.wpa[0:64, 320:384], psA[0:64, 2560:2624]),
                      after=[mmAvv], strict=[aAqr], tag="aAvv")
        asm = [
            p.emit("vector", lambda e: e.tensor_copy(self.wpa[0:64, 0:64], wsc[0:64, 1280:1344]),
                   after=[e_Aq], strict=[aAvv], tag="aAq"),
            p.emit("vector", lambda e: e.tensor_copy(self.wpa[0:64, 64:128], wsc2[0:64, 1280:1344]),
                   after=[e_Akv], tag="aAk"),
            p.emit("vector", lambda e: e.tensor_copy(self.wpa[0:64, 128:192], wb[0:64, 2112:2176]),
                   after=[e_kr], tag="aAkr"),
            aAq2, aAqr, aAvv,
        ]
        self.tok_wpa = p.dma("sync", lambda e: e.dma_start(out=self.wpa[64:128, :], in_=self.wpa[0:64, :]),
                             after=asm, tag="wpacp")
        if "setup" in self.taps:
            tw = self.tap("wpa", [128, 384], BF16)
            p.dma("sync", lambda e: e.dma_start(out=tw, in_=self.wpa[:]), after=[self.tok_wpa],
                  tag="tap_wpa")

    def phase1(self):
        p, st1, sqsc = self.p, self.st1, self.sqsc
        stats, last_sq, prev_by_buf = [], {}, {}
        for t in range(NT):
            buf = t % 2
            xs = self.x_stage[:, buf * 1024:(buf + 1) * 1024]
            dep = prev_by_buf[buf] if buf in prev_by_buf else [self.tok_wpa]
            ld = p.dma("sync", lambda e, t=t, xs=xs: e.dma_start(
                out=xs, in_=self.x_h[t * 128:(t + 1) * 128, :]), after=dep, tag=f"p1l{t}")
            sm = p.emit("vector", lambda e, t=t, xs=xs: e.reduce_sum(st1[:, t:t + 1], xs, axis=AX.X),
                        after=[ld], tag=f"p1s{t}")
            sq = p.emit("scalar", lambda e, t=t, xs=xs, buf=buf: e.activation(
                sqsc[:, buf * 1024:(buf + 1) * 1024], xs, AF.Square, accum_out=st1[:, 32 + t:33 + t]),
                after=[ld], strict=[last_sq[buf]] if buf in last_sq else [], tag=f"p1q{t}")
            last_sq[buf] = sq
            prev_by_buf[buf] = [sq, sm]
            stats += [sm, sq]
        self.tok_ln1 = self.lnstats_finalize(st1[:, 0:32], st1[:, 32:64], st1[:, 64:96],
                                             st1[:, 96:128], LN_EPS, stats, "p1")
        self.tok_p1sq = stats

    def chunk(self, c):
        p, st1 = self.p, self.st1
        buf = c % 2
        lc_v, lT_v = self.ln1c[buf], self.ln1T[buf]
        affs = []
        prev_half = None
        for hf in range(2):
            xs = self.x_stage[:, hf * 2048:(hf + 1) * 2048]
            dep = [self.tok_ln1]
            key = (c - 1, hf)
            if key in self.chunk_half_done:
                dep.append(self.chunk_half_done[key])
            ldc = p.dma("scalar", lambda e, c=c, hf=hf, xs=xs: e.dma_start(
                out=xs.rearrange("p (m l) -> p m l", l=128),
                in_=self.x_h.rearrange("(thi tlo) d -> tlo thi d", tlo=128)
                    [:, hf * 16:(hf + 1) * 16, c * 128:(c + 1) * 128]),
                after=dep, tag=f"c{c}x{hf}")
            for i in range(16):
                th = hf * 16 + i
                af = p.emit("vector", lambda e, i=i, th=th, xs=xs, lc_v=lc_v: e.tensor_scalar(
                    lc_v.rearrange("p (m l) -> p m l", l=128)[:, th, :],
                    xs.rearrange("p (m l) -> p m l", l=128)[:, i, :],
                    st1[:, 64 + th:65 + th], st1[:, 96 + th:97 + th], ALU.mult, ALU.add),
                    after=[ldc, self.tok_ln1] + (self.ln1c_users[buf] if th == 0 else []),
                    tag=f"c{c}a{th}")
                affs.append(af)
            self.chunk_half_done[(c, hf)] = affs[-1]
        trc = p.dma("sync", lambda e, lc_v=lc_v, lT_v=lT_v: e.dma_start(
            out=lT_v.rearrange("p (m l) -> p m l", l=128), in_=lc_v, transpose=True),
            after=affs + self.ln1T_users[buf], xpose=True, tag=f"c{c}tr")
        self.ln1c_users[buf] = [trc]
        self.ln1T_tok[buf] = trc
        self.ln1T_users[buf] = []
        if f"chunk{c}" in self.taps:
            tw = self.tap(f"ln1T{c}", [128, 4096], BF16)
            p.dma("sync", lambda e, tw=tw, lT_v=lT_v: e.dma_start(out=tw, in_=lT_v), after=[trc],
                  tag=f"tapc{c}")

    def prep_gen(self, h):
        p = self.p
        c, hh = h // 2, h % 2
        s0 = hh * 64
        buf = c % 2
        lT_v = self.ln1T[buf]
        psB = self.psB
        q_st, k_st = self.q_st, self.k_st
        v_st = self.v_st2[h % 2]
        stp, stx, nsc, rows = self.stp, self.stx, self.nsc, self.rows
        hp = self.hstate

        aqb = [self.sqsc[:, 0:64], self.sqsc[:, 64:128]]
        junk = self.sqsc[:, 128:256]
        kbuf = [self.sqsc[:, 256:384], self.sqsc[:, 384:512]]
        sqa_toks, sqq_toks, rdk_toks = [], [], []
        last_mm = last_ev3 = None
        for t_ in range(32):
            bu = t_ % 2
            dep = [self.ln1T_tok[buf], self.tok_wpa] + self.psB_chain
            mm = p.emit("tensor", lambda e, t_=t_, s0=s0, lT_v=lT_v: e.matmul(
                psB[:, 0:384], lT_v[s0:s0 + 64, t_ * 128:(t_ + 1) * 128],
                self.wpa[s0:s0 + 64, :], start=True, stop=True), after=dep, tag=f"h{h}p{t_}")
            last_mm = mm
            dep_ev = [mm] + ((hp.get("qk_free", []) + hp.get(f"vst_free{h % 2}", []))
                             if t_ == 0 else [])
            ev0 = p.emit("vector", lambda e, bu=bu: e.tensor_copy(
                aqb[bu], psB[:, 0:64]), after=dep_ev,
                strict=[sqa_toks[t_ - 2]] if t_ >= 2 else hp.get("prev_sqa", []),
                tag=f"h{h}ea{t_}")
            ev2 = p.emit("vector", lambda e, t_=t_: e.tensor_copy(
                k_st[:, t_ * 128:(t_ + 1) * 128], psB[:, 64:192]), after=dep_ev,
                tag=f"h{h}ek{t_}")
            ev1 = p.emit("vector", lambda e, t_=t_: e.tensor_copy(
                q_st[:, t_ * 128:(t_ + 1) * 128], psB[:, 192:320]), after=dep_ev,
                tag=f"h{h}eq{t_}")
            ev4 = p.emit("vector", lambda e, t_=t_, v_st=v_st: e.tensor_copy(
                v_st.rearrange("p (m l) -> p m l", l=65)[:, t_, 0:64], psB[:, 320:384]),
                after=dep_ev + [self.tok_vone_v[h % 2]], tag=f"h{h}ev{t_}")
            sqa = p.emit("scalar", lambda e, t_=t_, bu=bu: e.activation(
                junk[:, 0:64], aqb[bu], AF.Square, accum_out=stp[:, t_:t_ + 1]),
                after=[ev0] + (hp.get("stp_free", []) if t_ == 0 else []), tag=f"h{h}qa{t_}")
            sqa_toks.append(sqa)
            sqq = p.emit("scalar", lambda e, t_=t_: e.activation(
                junk, q_st[:, t_ * 128:(t_ + 1) * 128], AF.Square,
                accum_out=stp[:, 32 + t_:33 + t_]),
                after=[ev1], tag=f"h{h}qq{t_}")
            sqq_toks.append(sqq)
            sqk = p.emit("scalar", lambda e, t_=t_, bu=bu: e.activation(
                kbuf[bu], k_st[:, t_ * 128:(t_ + 1) * 128], AF.Square),
                after=[ev2], strict=[rdk_toks[t_ - 2]] if t_ >= 2 else hp.get("prev_rdk", []),
                tag=f"h{h}qk{t_}")
            rdk = p.emit("vector", lambda e, t_=t_, bu=bu: e.reduce_sum(
                stp[:, 64 + 2 * t_:66 + 2 * t_], kbuf[bu].rearrange("p (m l) -> p m l", l=64),
                axis=AX.X), after=[sqk],
                strict=hp.get("stp_free", []) if t_ == 0 else [], tag=f"h{h}rk{t_}")
            rdk_toks.append(rdk)
            self.psB_chain = [ev4]
            last_ev3 = ev4
            yield
        hp["prev_prep_sq"] = rdk_toks[-1]
        hp["prev_sqa"] = [sqa_toks[-1], sqa_toks[-2]]
        hp["prev_rdk"] = [rdk_toks[-1], rdk_toks[-2]]

        kview = stp[:, 64:128].rearrange("p (m l) -> p m l", l=2)
        stq = stp[:, 160:192]
        stk = stx[:, 64:96]
        tqa, tka = self.sqsc[:, 2496:2528], self.sqsc[:, 2528:2560]
        r2q = self.gst[:, 0:32]
        r2k = self.gst[:, 32:64]
        t1 = p.emit("vector", lambda e: e.tensor_scalar(tqa, stp[:, 0:32], 1.0 / 64, RMS_EPS,
                                                        ALU.mult, ALU.add),
                    after=[sqa_toks[-1], sqq_toks[-1], rdk_toks[-1]],
                    strict=[rdk_toks[-1]], tag=f"h{h}tq")
        t2 = p.emit("vector", lambda e: e.tensor_scalar(tka, kview[:, :, 0], 1.0 / 64, RMS_EPS,
                                                        ALU.mult, ALU.add), strict=[t1], tag=f"h{h}tk")
        g1 = p.emit("vector", lambda e: e.reciprocal(r2q, tqa), strict=[t2],
                    after=hp.get("gst_free", []), tag=f"h{h}rq")
        g2 = p.emit("vector", lambda e: e.reciprocal(r2k, tka), strict=[g1], tag=f"h{h}rk")
        nrq = self.newton_rsqrt(stq, tqa, 32, [g2], tagp=f"h{h}q")
        s125 = p.emit("vector", lambda e: e.tensor_scalar(stq, stq, 0.125, None, ALU.mult),
                      strict=[nrq], tag=f"h{h}q125")
        nrk = self.newton_rsqrt(stk, tka, 32, [s125], tagp=f"h{h}k")
        yield

        sc_toks = []
        for t_ in range(NT):
            a_ = p.emit("vector", lambda e, t_=t_: e.tensor_scalar(
                q_st.rearrange("p (m l) -> p m l", l=128)[:, t_, :],
                q_st.rearrange("p (m l) -> p m l", l=128)[:, t_, :],
                stq[:, t_:t_ + 1], None, ALU.mult),
                after=[s125], strict=[last_ev3] if t_ == 0 else [], tag=f"h{h}sq{t_}")
            b_ = p.emit("vector", lambda e, t_=t_: e.tensor_scalar(
                k_st.rearrange("p (m l) -> p m l", l=128)[:, t_, 0:64],
                k_st.rearrange("p (m l) -> p m l", l=128)[:, t_, 0:64],
                stk[:, t_:t_ + 1], None, ALU.mult),
                after=[nrk], tag=f"h{h}sk{t_}")
            sc_toks += [a_, b_]
            if t_ % 8 == 7:
                yield
        qT, kT = self.qT_v[h % 2], self.kT_v[h % 2]
        trq = p.dma("sync", lambda e, qT=qT: e.dma_start(
            out=qT.rearrange("p (m l) -> p m l", l=128), in_=q_st, transpose=True),
            after=sc_toks + hp.get(f"qT_free{h % 2}", []), xpose=True, tag=f"h{h}Tq")
        trk = p.dma("sync", lambda e, kT=kT: e.dma_start(
            out=kT.rearrange("p (m l) -> p m l", l=128), in_=k_st, transpose=True),
            after=sc_toks + hp.get(f"kT_free{h % 2}", []), xpose=True, tag=f"h{h}Tk")

        q2m = p.emit("vector", lambda e: e.tensor_mul(nsc[:, 0:32], stp[:, 32:64], r2q),
                     strict=[nrk], tag=f"h{h}q2b")
        k2s = p.emit("vector", lambda e: e.tensor_mul(nsc[:, 32:64], kview[:, :, 0], r2k),
                     strict=[q2m], tag=f"h{h}k2a")
        k2t = p.emit("vector", lambda e: e.tensor_add(nsc[:, 32:64], nsc[:, 32:64], kview[:, :, 1]),
                     strict=[k2s], tag=f"h{h}k2b")
        mq = p.emit("vector", lambda e: e.reduce_max(nsc[:, 62:63], nsc[:, 0:32], axis=AX.X),
                    strict=[k2t], tag=f"h{h}mq")
        mk = p.emit("vector", lambda e: e.reduce_max(nsc[:, 63:64], nsc[:, 32:64], axis=AX.X),
                    strict=[mq], tag=f"h{h}mk")
        pr = p.emit("vector", lambda e: e.tensor_mul(nsc[:, 62:63], nsc[:, 62:63], nsc[:, 63:64]),
                    strict=[mk], tag=f"h{h}pp")
        pr2 = p.emit("vector", lambda e: e.tensor_scalar(nsc[:, 62:63], nsc[:, 62:63], 0.015625, 1e-30,
                                                         ALU.mult, ALU.add), strict=[pr], tag=f"h{h}p2")
        trm = p.emit("tensor", lambda e: e.transpose(psB[0:1, 0:128], nsc[:, 62:63], self.identf[:]),
                     after=[pr2, self.tok_id] + self.psB_chain, tag=f"h{h}tm")
        evm = p.emit("vector", lambda e: e.tensor_copy(rows[0:1, 0:128], psB[0:1, 0:128]),
                     after=[trm], strict=[pr2] + hp.get("rows_free", []), tag=f"h{h}em")
        rmx = p.emit("vector", lambda e: e.reduce_max(rows[0:1, 128:129], rows[0:1, 0:128], axis=AX.X),
                     strict=[evm], tag=f"h{h}rm")
        nrm = self.newton_rsqrt(rows[0:1, 129:130], rows[0:1, 128:129], 1, [rmx], tagp=f"h{h}M")
        msq = p.emit("vector", lambda e: e.tensor_mul(rows[0:1, 130:131], rows[0:1, 128:129],
                                                      rows[0:1, 129:130]), strict=[nrm], tag=f"h{h}ms")
        mng = p.emit("vector", lambda e: e.tensor_scalar(rows[0:1, 131:132], rows[0:1, 130:131], -1.0,
                                                         None, ALU.mult), strict=[msq], tag=f"h{h}mg")
        bcm = p.emit("tensor", lambda e: e.matmul(psB[:, 128:129], self.onesf[0:1, 0:128],
                                                  rows[0:1, 131:132], start=True, stop=True),
                     after=[mng, self.tok_onesf, trm], tag=f"h{h}bc")
        evbm = p.emit("vector", lambda e, h=h: e.tensor_copy(self.biasM[:, h % 2:h % 2 + 1],
                                                             psB[:, 128:129]),
                      after=[bcm] + hp.get(f"biasM_free{h % 2}", []), strict=[mng], tag=f"h{h}eb")
        self.psB_chain = [evbm]
        hp["rows_free"] = [mng]
        hp["gst_free"] = [k2s]
        hp["stp_free"] = [k2t]
        hp["qk_free"] = [trq, trk]
        hp[f"prep{h}"] = (trq, trk, evbm)
        self.ln1T_users[buf] += [last_mm]
        yield

    def qc_gen(self, h):
        p = self.p
        c, hh = h // 2, h % 2
        s0 = hh * 64
        psA = self.psA
        qT, kT = self.qT_v[h % 2], self.kT_v[h % 2]
        v_st = self.v_st2[h % 2]
        ebuf = self.ebuf
        hp = self.hstate
        trq, trk, evbm = hp.pop(f"prep{h}")
        grp_exp = hp.setdefault("grp_exp", {0: None, 1: None})
        grp_av = hp.setdefault("grp_av", {0: None, 1: None})
        gidx = hp.get("gidx", 0)
        uev_prev, dev_prev = None, None
        last_exp = last_av = None
        pend = None

        def emit_av(pend):
            kts, gbuf, ex, mtoks, pnkt = pend
            ctx_l = None
            for i, kt in enumerate(kts):
                dep = [ex] + mtoks
                if kt == 0:
                    dep += [t for t in (hp.get("uev_prev"), hp.get("dev_prev")) if t is not None]
                    dep += [self.tok_vone_v[h % 2]] + hp.pop("ctx_init", [])
                ctx_l = p.emit("tensor", lambda e, i=i, kt=kt, gbuf=gbuf,
                               s=(kt == 0), st=(kt == pnkt - 1): e.matmul(
                    self.psA[0:65, 3072:3584],
                    v_st.rearrange("p (m l) -> p m l", l=65)[:, kt, :],
                    ebuf[:, gbuf * 1536 + i * 512:gbuf * 1536 + (i + 1) * 512],
                    start=s, stop=st), after=dep, tag=f"h{h}a{kt}")
            grp_av[gbuf] = ctx_l
            return ctx_l

        for qc in range(8):
            nkt = 4 * qc + 4
            groups = [list(range(g0, min(g0 + 3, nkt))) for g0 in range(0, nkt, 3)]
            ctx_last = None
            for kts in groups:
                gbuf = gidx % 2
                gidx += 1
                base = gbuf * 1536
                smms = []
                for i, kt in enumerate(kts):
                    dep = [trq, trk]
                    if i == 0 and grp_exp[gbuf] is not None:
                        dep.append(grp_exp[gbuf])
                    elif i == 0:
                        dep.append(self.tok_lncols)
                    if i == 0 and grp_av[gbuf] is not None:
                        dep.append(grp_av[gbuf])
                    smms.append(p.emit("tensor", lambda e, kt=kt, i=i, base=base, qc=qc: e.matmul(
                        psA[:, base + i * 512:base + (i + 1) * 512],
                        kT[:, kt * 128:(kt + 1) * 128], qT[:, qc * 512:(qc + 1) * 512],
                        start=True, stop=True), after=dep, tag=f"h{h}s{qc}_{kt}"))
                edep = smms + [evbm]
                if grp_av[gbuf] is not None:
                    edep.append(grp_av[gbuf])
                ex = p.emit("scalar", lambda e, n=len(kts), base=base, gbuf=gbuf, h=h: e.activation(
                    ebuf[:, gbuf * 1536:gbuf * 1536 + n * 512], psA[:, base:base + n * 512],
                    AF.Exp, bias=self.biasM[:, h % 2:h % 2 + 1]), after=edep, tag=f"h{h}x{qc}")
                grp_exp[gbuf] = ex
                last_exp = ex
                mtoks = []
                for i, kt in enumerate(kts):
                    if kt >= 4 * qc:
                        rr = kt - 4 * qc
                        mtoks.append(p.emit("vector", lambda e, i=i, rr=rr, gbuf=gbuf: e.tensor_mul(
                            ebuf[:, gbuf * 1536 + i * 512:gbuf * 1536 + (i + 1) * 512],
                            ebuf[:, gbuf * 1536 + i * 512:gbuf * 1536 + (i + 1) * 512],
                            self.cmask[:, rr * 512:(rr + 1) * 512]),
                            after=[ex, self.tok_cmask], tag=f"h{h}k{qc}_{kt}"))
                if pend is not None:
                    ctx_last = emit_av(pend)
                pend = (kts, gbuf, ex, mtoks, nkt)
                if kts[-1] == nkt - 1:
                    ctx_last = emit_av(pend)
                    pend = None
                last_av = ctx_last if ctx_last is not None else last_av
                yield
            uev = p.emit("vector", lambda e, qc=qc: e.tensor_copy(
                self.uh[0:65, qc * 512:(qc + 1) * 512], self.psA[0:65, 3072:3584]),
                after=[ctx_last] + ([hp.get("uh_dma"), hp.get("den_dma")] if qc == 0 else []),
                tag=f"h{h}u{qc}")
            hp["uev_prev"] = uev
            uev_prev = uev
            yield
        hp["gidx"] = gidx
        ust = p.dma("sync", lambda e, c=c, s0=s0: e.dma_start(
            out=self.ut_h[c * 128 + s0:c * 128 + s0 + 64, :],
            in_=self.uh[0:64, 0:4096]), after=[uev_prev], tag=f"h{h}U")
        dend = p.dma("sync", lambda e, h=h: e.dma_start(
            out=self.den16[h:h + 1, 0:4096], in_=self.uh[64:65, 0:4096]), after=[uev_prev],
            tag=f"h{h}D")
        hp["uh_dma"] = ust
        hp["den_dma"] = dend
        self.den_toks.append(dend)
        hp[f"qT_free{h % 2}"] = [last_av]
        hp[f"kT_free{h % 2}"] = [last_av]
        hp[f"vst_free{h % 2}"] = [last_av]
        hp[f"biasM_free{h % 2}"] = [last_exp]
        self.ust_toks.append(ust)
        self.tok_last_av = last_av
        self.tok_last_exp = last_exp
        if f"head{h}" in self.taps:
            tw = self.tap(f"uh{h}", [64, 4096], BF16)
            p.dma("sync", lambda e, tw=tw: e.dma_start(
                out=tw, in_=self.uh[0:64, 0:4096]), after=[uev_prev], tag=f"tapu{h}")

    def phase4_gen(self):
        p, st1, sqsc, a_tok = self.p, self.st1, self.sqsc, self.a_tok
        self.wld0 = p.dma("scalar", lambda e: e.dma_start(
            out=self.wst_fc[0].rearrange("p (dh j) -> p dh j", dh=8),
            in_=self.wfcq_h.rearrange("(dh dl) j -> dl dh j", dl=128)[:, :, 0:512]),
            after=self.tok_quant, tag="Fw0pre")
        trden = p.dma("sync", lambda e: e.dma_start(
            out=self.denT.rearrange("p (m l) -> p m l", l=16), in_=self.den16[:, 0:4096],
            transpose=True), after=self.den_toks, xpose=True, tag="p4td")
        wait_pool = self.ust_toks + self.hstate["qk_free"] + [self.tok_last_av] + \
            self.ln1c_users[0] + self.ln1c_users[1] + [self.tok_last_exp]
        atr = []
        for c in range(NCH):
            atr.append(p.dma("sync", lambda e, c=c: e.dma_start(
                out=a_tok.rearrange("p (m l) -> p m l", l=1024)[:, :, c * 128:(c + 1) * 128],
                in_=self.ut_h[c * 128:(c + 1) * 128, :], transpose=True),
                after=wait_pool, xpose=True, tag=f"p4a{c}"))
        rc_all = p.emit("vector", lambda e: e.reciprocal(sqsc[:, 2048:2560], self.denT[:, 0:512]),
                        after=[trden, self.hstate["prev_prep_sq"]], tag="p4rc")
        rcv = sqsc[:, 2048:2560].rearrange("p (m l) -> p m l", l=16)
        atv = a_tok.rearrange("p (m l) -> p m l", l=1024)
        self.x1_store = []
        self.tr_toks = []
        prev_sq = None
        prev_fin = [rc_all]
        prev_af = None
        ln3stats = {}
        aff3 = {}
        yield
        for g in range(4):
            stats = []
            for t_ in range(8 * g, 8 * g + 8):
                norm = None
                for hh in range(H):
                    norm = p.emit("vector", lambda e, t_=t_, hh=hh: e.tensor_scalar(
                        atv[:, t_, hh * 64:(hh + 1) * 64], atv[:, t_, hh * 64:(hh + 1) * 64],
                        rcv[:, t_, hh:hh + 1], None, ALU.mult),
                        after=atr + [rc_all], tag=f"p4n{t_}_{hh}")
                sm = p.emit("vector", lambda e, t_=t_: e.reduce_sum(
                    st1[:, 128 + t_:129 + t_], atv[:, t_, :], axis=AX.X),
                    after=[norm], tag=f"p4s{t_}")
                sq = p.emit("scalar", lambda e, t_=t_: e.activation(
                    sqsc[:, (t_ % 2) * 1024:(t_ % 2) * 1024 + 1024], atv[:, t_, :], AF.Square,
                    accum_out=st1[:, 160 + t_:161 + t_]),
                    after=[norm], strict=([prev_sq] if prev_sq else []) +
                    [self.hstate["prev_prep_sq"]], tag=f"p4q{t_}")
                prev_sq = sq
                stats += [sm, sq]
            fin2 = self.lnstats_finalize(st1[:, 128 + 8 * g:136 + 8 * g],
                                         st1[:, 160 + 8 * g:168 + 8 * g],
                                         st1[:, 64 + 8 * g:72 + 8 * g],
                                         st1[:, 96 + 8 * g:104 + 8 * g],
                                         LN_EPS, stats + prev_fin, f"p4f{g}")
            yield
            for t_ in range(8 * g, 8 * g + 8):
                bu = t_ % 2
                th = t_ % 8
                sl = self.x1slots[th]
                lx = p.dma("gpsimd", lambda e, t_=t_, sl=sl: e.dma_start(
                    out=sl, in_=self.x_h[t_ * 128:(t_ + 1) * 128, :]),
                    after=(self.tok_quant + [trden, rc_all,
                                             self.chunk_half_done[(NCH - 1, 1)],
                                             self.chunk_half_done[(NCH - 2, 1)]])
                    if t_ < 8 else
                    [aff3[t_ - 8], self.x1_store[t_ - 8], ln3stats[t_ - 8]],
                    tag=f"p4x{t_}")
                af = p.emit("vector", lambda e, t_=t_, bu=bu: e.tensor_scalar(
                    sqsc[:, bu * 1024:bu * 1024 + 1024], atv[:, t_, :],
                    st1[:, 64 + t_:65 + t_], st1[:, 96 + t_:97 + t_], ALU.mult, ALU.add),
                    after=[fin2, prev_sq], strict=[], tag=f"p4f{t_}")
                prev_af = af
                ad = p.emit("vector", lambda e, t_=t_, bu=bu, sl=sl: e.tensor_add(
                    sl, sl, sqsc[:, bu * 1024:bu * 1024 + 1024]),
                    after=[lx], strict=[af], tag=f"p4d{t_}")
                stx_ = p.dma("sync", lambda e, t_=t_, sl=sl: e.dma_start(
                    out=self.x1_h[t_ * 128:(t_ + 1) * 128, :], in_=sl),
                    after=[ad], tag=f"p4w{t_}")
                self.x1_store.append(stx_)
                sm3 = p.emit("vector", lambda e, t_=t_, sl=sl: e.reduce_sum(
                    st1[:, t_:t_ + 1], sl, axis=AX.X),
                    strict=[ad], tag=f"p4l{t_}")
                sq3 = p.emit("scalar", lambda e, t_=t_, sl=sl: e.activation(
                    sqsc[:, ((t_ + 1) % 2) * 1024:((t_ + 1) % 2) * 1024 + 1024], sl,
                    AF.Square, accum_out=st1[:, 32 + t_:33 + t_]),
                    after=[ad], tag=f"p4m{t_}")
                ln3stats[t_] = sq3
                prev_sq = sq3
                last_sm3 = sm3
            fin3 = self.lnstats_finalize(st1[:, 8 * g:8 + 8 * g], st1[:, 32 + 8 * g:40 + 8 * g],
                                         st1[:, 192 + 8 * g:200 + 8 * g],
                                         st1[:, 224 + 8 * g:232 + 8 * g],
                                         LN_EPS, [last_sm3, ln3stats[8 * g + 7],
                                                  ln3stats[8 * g + 6],
                                                  prev_af, fin2], f"p4g{g}")
            prev_fin = [fin3]
            yield
            for t in range(8 * g, 8 * g + 8):
                th = t % 8
                sl = self.x1slots[th]
                dep = [fin3]
                if th == 0 and self.tr_toks:
                    dep.append(self.tr_toks[-1])
                af3 = p.emit("vector", lambda e, t=t, th=th, sl=sl: e.tensor_scalar(
                    self.ln3tok[:, th * 1024:(th + 1) * 1024], sl,
                    st1[:, 192 + t:193 + t], st1[:, 224 + t:225 + t], ALU.mult, ALU.add),
                    after=dep, strict=[], tag=f"La{t}")
                aff3[t] = af3
            trdep = [aff3[8 * g + 6], aff3[8 * g + 7]]
            tr = p.dma("sync", lambda e, g=g: e.dma_start(
                out=self.ln3T.rearrange("p (m l) -> p m l", l=128)
                    .rearrange("p (g m) l -> p g m l", g=4)[:, g],
                in_=self.ln3tok[:, 0:8192], transpose=True),
                after=trdep, xpose=True, tag=f"Ltr{g}")
            self.tr_toks.append(tr)
            yield
        self.p4_last = [prev_af, prev_sq]
        self.aff3_last = [aff3[30], aff3[31]]
        if "x1" in self.taps:
            tw = self.tap("x1", [T, D], F32)
            p.dma("sync", lambda e, tw=tw: e.dma_start(out=tw, in_=self.x1_h[:]),
                  after=self.x1_store, tag="tapx1")

    def quant_prep(self):
        p, psA, identf = self.p, self.psA, self.identf
        lw = p.dma("sync", lambda e: e.dma_start(out=self.lnw_row,
                                                 in_=self.ln_w.rearrange("(a b) -> a b", a=1)),
                   after=[self.tok_wpa], tag="q_lw")
        prev_tt = {}

        def row_to_cols(row_tok, dstc, tagp):
            toks = []
            for g in range(32):
                bank = g % 4
                dep = [row_tok, self.tok_id]
                if (tagp, bank) in prev_tt:
                    dep.append(prev_tt[(tagp, bank)])
                tt = p.emit("tensor", lambda e, g=g, bank=bank: e.transpose(
                    psA[:, bank * 512:bank * 512 + 1], self.lnw_row[0:1, g * 128:(g + 1) * 128],
                    identf[0:1, 0:1]), after=dep, tag=f"{tagp}t{g}")
                ev = p.emit("vector", lambda e, g=g, bank=bank, dstc=dstc: e.tensor_copy(
                    dstc[:, g:g + 1], psA[:, bank * 512:bank * 512 + 1]), after=[tt],
                    strict=toks[-1:], tag=f"{tagp}e{g}")
                prev_tt[(tagp, bank)] = ev
                toks.append(ev)
            return toks

        lnw_toks = row_to_cols(lw, self.lnw_c, "qlw")
        lb = p.dma("sync", lambda e: e.dma_start(out=self.lnw_row,
                                                 in_=self.ln_b.rearrange("(a b) -> a b", a=1)),
                   after=lnw_toks, tag="q_lb")
        lnb_toks = row_to_cols(lb, self.lnb_c, "qlb")
        self.tok_lncols = lnb_toks[-1]
        self.tok_lb = lb

    def quant_units(self):
        p, psB, identf = self.p, self.psB, self.identf
        seg_toks = []
        for gsg in range(8):
            cs = p.emit("vector", lambda e, gsg=gsg: e.tensor_copy(
                self.qw_sgn[0:1, 0:512], self.lnw_row[0:1, gsg * 512:(gsg + 1) * 512]),
                after=[self.tok_lb, self.tok_lncols] + seg_toks[-1:], tag=f"qbs{gsg}")
            bc = p.emit("tensor", lambda e: e.matmul(
                psB[:, 0:512], self.onesb[0:1, 0:128], self.qw_sgn[0:1, 0:512],
                start=True, stop=True),
                after=[cs, self.tok_ones] + seg_toks[-1:] + self.psB_chain, tag=f"qbb{gsg}")
            ev = p.emit("vector", lambda e, gsg=gsg: e.tensor_copy(
                self.lnbB[:, gsg * 512:(gsg + 1) * 512], psB[:, 0:512]), after=[bc],
                strict=[cs], tag=f"qbe{gsg}")
            seg_toks.append(ev)
            self.psB_chain = [ev]
        yield
        marks = {}
        for widx, (w_hbm, nr_, nc2) in enumerate(((self.W_fc, 4096, 1024),
                                                  (self.W_proj, 1024, 4096))):
            if widx == 0:
                sview4 = None
                sview = w_hbm.rearrange("(s p) n -> s p n", p=128)
            else:
                sview4 = w_hbm.rearrange("(dc p) (jq n) -> dc jq p n", p=128, n=1024)
                sview = None
            acc = self.gst[:, 64 + widx:65 + widx]
            tmps = [self.gst[:, 66:67], self.gst[:, 67:68]]
            hcol = self.gst[:, 68 + widx:69 + widx]
            acc_tok = None
            red_prev = None
            use_prev = {0: None, 1: None}
            for s in range(32):
                bu = s % 2
                dst = self.qw_f32[bu]
                srcap = sview[s] if sview is not None else sview4[s // 4, s % 4]
                ld = p.dma("gpsimd", lambda e, dst=dst, srcap=srcap: e.dma_start(
                    out=dst, in_=srcap),
                    after=[use_prev[bu]] if use_prev[bu] is not None else
                    [self.tok_lncols, seg_toks[-1]],
                    tag=f"g{widx}l{s}")
                rd = p.emit("vector", lambda e, dst=dst, tmp=tmps[bu]: e.reduce_sum(
                    tmp, dst, axis=AX.X, apply_absolute_value=True),
                    after=[ld], strict=[red_prev] if red_prev else [], tag=f"g{widx}r{s}")
                if s == 0:
                    acc_tok = p.emit("vector", lambda e, tmp=tmps[bu], acc=acc: e.tensor_copy(
                        acc, tmp), strict=[rd], tag=f"g{widx}a{s}")
                else:
                    acc_tok = p.emit("vector", lambda e, tmp=tmps[bu], acc=acc: e.tensor_add(
                        acc, acc, tmp), strict=[rd, acc_tok], tag=f"g{widx}a{s}")
                red_prev = rd
                use_prev[bu] = rd
                if s % 4 == 3:
                    yield
            trg = p.emit("tensor", lambda e, acc=acc: e.transpose(psB[0:1, 0:128], acc,
                                                                  identf[:]),
                         after=[acc_tok, self.tok_id, seg_toks[-1]] + self.psB_chain,
                         tag=f"g{widx}t")
            evg = p.emit("vector", lambda e: e.tensor_copy(self.rows[0:1, 256:384], psB[0:1, 0:128]),
                         after=[trg], strict=[acc_tok], tag=f"g{widx}e")
            rs = p.emit("vector", lambda e: e.reduce_sum(self.rows[0:1, 384:385],
                                                         self.rows[0:1, 256:384], axis=AX.X),
                        strict=[evg], tag=f"g{widx}s")
            nel = float(nr_ * nc2)
            h1 = p.emit("vector", lambda e, nel=nel: e.tensor_scalar(
                self.rows[0:1, 385:386], self.rows[0:1, 384:385], 0.5 / nel, 0.5 * ABS_EPS,
                ALU.mult, ALU.add), strict=[rs], tag=f"g{widx}h")
            bch = p.emit("tensor", lambda e: e.matmul(psB[:, 128:129], self.onesf[0:1, 0:128],
                                                      self.rows[0:1, 385:386], start=True,
                                                      stop=True),
                         after=[h1, self.tok_onesf, trg], tag=f"g{widx}b")
            evh = p.emit("vector", lambda e, hcol=hcol: e.tensor_copy(hcol, psB[:, 128:129]),
                         after=[bch], strict=[h1], tag=f"g{widx}eh")
            hncol = self.gst[:, 70 + widx:71 + widx]
            nh = p.emit("vector", lambda e, hcol=hcol, hncol=hncol: e.tensor_scalar(
                hncol, hcol, -1.0, None, ALU.mult), strict=[evh], tag=f"g{widx}nh")
            self.psB_chain = [evg, evh]
            yield
            sq_prev = None
            ya_prev = None
            wq_extra = []
            for s in range(32):
                bu = s % 2
                dst = self.qw_f32[bu]
                srcap = sview[s] if sview is not None else sview4[s // 4, s % 4]
                ld = p.dma("gpsimd", lambda e, dst=dst, srcap=srcap: e.dma_start(
                    out=dst, in_=srcap),
                    after=[evh] if s < 2 else [marks[(widx, s - 2)]], tag=f"t{widx}l{s}")
                d1 = p.emit("vector", lambda e, dst=dst, hcol=hcol: e.tensor_scalar(
                    self.qw_ind[:, 0:1024], dst, hcol, None, ALU.is_gt),
                    after=[ld, nh], strict=[sq_prev] if sq_prev else [], tag=f"t{widx}in{s}")
                d2 = p.emit("vector", lambda e, dst=dst, hncol=hncol: e.tensor_scalar(
                    self.qw_sq[:, 0:1024], dst, hncol, None, ALU.is_lt),
                    strict=[d1], tag=f"t{widx}il{s}")
                wq = p.emit("vector", lambda e: e.tensor_sub(self.qw_q[:, 0:1024],
                                                             self.qw_ind[:, 0:1024],
                                                             self.qw_sq[:, 0:1024]),
                            after=wq_extra, strict=[d2], tag=f"t{widx}wq{s}")
                sq_prev = wq
                trb = self.qw_tr[bu]
                trw = p.dma("sync", lambda e, trb=trb: e.dma_start(
                    out=trb.rearrange("p (m l) -> p m l", l=128), in_=self.qw_q[:, 0:1024],
                    transpose=True),
                    after=[wq] + ([marks[(widx, s - 2)]] if s >= 2 else []), xpose=True,
                    tag=f"t{widx}tr{s}")
                wq_extra = [trw]
                if widx == 0:
                    stw = p.dma("gpsimd", lambda e, s=s, trb=trb: e.dma_start(
                        out=self.wfcq_h.rearrange("(dh dl) (sj jl) -> dl dh sj jl",
                                                  dl=128, jl=128)[:, :, s, :],
                        in_=trb.rearrange("p (dh jl) -> p dh jl", jl=128)), after=[trw],
                        tag=f"t{widx}st{s}")
                else:
                    dc, jq = s // 4, s % 4
                    ybt = self.ybB_t[:, 1032 + bu:1033 + bu]
                    stt = p.emit("vector", lambda e, jq=jq, ybt=ybt: e.scalar_tensor_tensor(
                        self.qw_sq[:, 0:512].bitcast(BF16)[:, 0:1024], self.qw_q[:, 0:1024],
                        1.0, self.lnbB[:, jq * 1024:(jq + 1) * 1024], ALU.mult, ALU.mult,
                        accum_out=ybt), after=[seg_toks[-1]],
                        strict=[wq] + ([ya_prev] if ya_prev else []), tag=f"yb{s}")
                    wq_extra.append(stt)
                    ya = p.emit("vector", lambda e, dc=dc, ybt=ybt, init=(jq == 0), s=s:
                                (e.tensor_copy(self.ybB_t[:, 1024 + dc:1025 + dc], ybt)
                                 if init else
                                 e.tensor_add(self.ybB_t[:, 1024 + dc:1025 + dc],
                                              self.ybB_t[:, 1024 + dc:1025 + dc], ybt)),
                                strict=[stt], tag=f"yba{s}")
                    ya_prev = ya
                    folds = []
                    for j_hi in range(8):
                        col = jq * 8 + j_hi
                        folds.append(p.emit("vector", lambda e, col=col, j_hi=j_hi, trb=trb:
                                            e.tensor_scalar(
                            trb[:, j_hi * 128:(j_hi + 1) * 128],
                            trb[:, j_hi * 128:(j_hi + 1) * 128],
                            self.lnw_c[:, col:col + 1], None, ALU.mult),
                            after=[trw, self.tok_lncols],
                            strict=[ya] + (folds[-1:] if folds else []),
                            tag=f"fw{s}_{j_hi}"))
                    stw = p.dma("gpsimd", lambda e, dc=dc, jq=jq, trb=trb: e.dma_start(
                        out=self.wprq_h.rearrange("(jq jh jl) (dc dl) -> jq dc jl jh dl",
                                                  jl=128, dl=128, jh=8)[jq, dc],
                        in_=trb.rearrange("p (jh dl) -> p jh dl", dl=128)),
                        after=folds, tag=f"t{widx}st{s}")
                marks[(widx, s)] = stw
                if s % 2 == 1:
                    yield
        ybdone = []
        for dc in range(8):
            trp = p.emit("tensor", lambda e, dc=dc: e.transpose(
                psB[0:1, 256:384], self.ybB_t[:, 1024 + dc:1025 + dc], identf[:]),
                after=[ya_prev, self.tok_id] + ybdone[-1:] + self.psB_chain, tag=f"ybt{dc}")
            evr = p.emit("vector", lambda e: e.tensor_copy(self.rows[0:1, 384:512],
                                                           psB[0:1, 256:384]),
                         after=[trp], strict=[ya_prev] + ybdone[-1:], tag=f"ybe{dc}")
            bcp = p.emit("tensor", lambda e: e.matmul(psB[:, 384:512], self.onesf[0:1, 0:128],
                                                      self.rows[0:1, 384:512], start=True,
                                                      stop=True),
                         after=[evr, self.tok_onesf], tag=f"ybb{dc}")
            evo = p.emit("vector", lambda e, dc=dc: e.tensor_copy(
                self.ybB_t[:, dc * 128:(dc + 1) * 128], psB[:, 384:512]), after=[bcp],
                strict=[evr], tag=f"ybo{dc}")
            ybdone.append(evo)
            self.psB_chain = [evr, evo]
        self.tok_quant = [marks[(w, s)] for w in (0, 1) for s in range(32)] + [ybdone[-1]]
        if "quant" in self.taps:
            tw = self.tap("wfcq", [D, 4096], BF16)
            p.dma("sync", lambda e, tw=tw: e.dma_start(out=tw, in_=self.wfcq_h[:]),
                  after=self.tok_quant, tag="tapwf")
            tw2 = self.tap("wprq", [4096, D], BF16)
            p.dma("sync", lambda e, tw2=tw2: e.dma_start(out=tw2, in_=self.wprq_h[:]),
                  after=self.tok_quant, tag="tapwp")
            tw3 = self.tap("ybB", [128, 1024], F32)
            p.dma("sync", lambda e, tw3=tw3: e.dma_start(out=tw3, in_=self.ybB_t[:, 0:1024]),
                  after=self.tok_quant, tag="tapyb")
        yield

    def fc_gen(self):
        p, st1, sqsc, psA = self.p, self.st1, self.sqsc, self.psA
        tr_toks = self.tr_toks
        gsum, g2 = self.gst[:, 0:32], self.gst[:, 32:64]
        gt0, gt1 = self.gst[:, 70:71], self.gst[:, 71:72]
        hst_toks = {}
        wld_prev = {0: None, 1: None}
        gelu_prev = None
        gsq_prev = None
        gadd_prev = None
        mm_last = None
        for jc in range(8):
            bu = jc % 2
            if jc == 0:
                wld = self.wld0
            else:
                wld = p.dma("sync", lambda e, jc=jc, bu=bu: e.dma_start(
                    out=self.wst_fc[bu].rearrange("p (dh j) -> p dh j", dh=8),
                    in_=self.wfcq_h.rearrange("(dh dl) j -> dl dh j", dl=128)
                        [:, :, jc * 512:(jc + 1) * 512]),
                    after=([wld_prev[bu]] if wld_prev[bu] is not None else
                           self.tok_quant + tr_toks[-1:]),
                    tag=f"Fw{jc}")
            for t in range(NT):
                pb = t % 2
                mms = []
                for dh in range(8):
                    dep = [wld, tr_toks[t // 8]]
                    if dh == 0:
                        if t >= 2:
                            dep.append(hst_toks.get((jc, t - 2, "gelu")))
                        elif jc > 0:
                            dep.append(hst_toks.get((jc - 1, NT - 2 + t, "gelu")))
                        if jc == 0 and t < 2:
                            dep += [self.tok_last_av]
                    mms.append(p.emit("tensor", lambda e, t=t, dh=dh, pb=pb, bu=bu: e.matmul(
                        psA[:, pb * 512:(pb + 1) * 512],
                        self.ln3T[:, (t * 8 + dh) * 128:(t * 8 + dh) * 128 + 128],
                        self.wst_fc[bu][:, dh * 512:(dh + 1) * 512],
                        start=(dh == 0), stop=(dh == 7)), after=dep, tag=f"F{jc}m{t}_{dh}"))
                mm_last = mms[-1]
                ge = p.emit("scalar", lambda e, t=t, pb=pb: e.activation(
                    self.hbuf[:, t * 512:(t + 1) * 512], psA[:, pb * 512:(pb + 1) * 512],
                    AF.Gelu, accum_out=gt0 if t % 2 == 0 else gt1),
                    after=[mms[-1], gadd_prev] + ([hst_toks[(jc - 1, "dma")]] if jc > 0 and t == 0
                                                  else []),
                    strict=[gelu_prev] if gelu_prev else [], tag=f"F{jc}g{t}")
                gelu_prev = ge
                hst_toks[(jc, t, "gelu")] = ge
                if jc == 0:
                    ga = p.emit("vector", lambda e, t=t: e.tensor_copy(
                        gsum[:, t:t + 1], gt0 if t % 2 == 0 else gt1), after=[ge],
                        strict=[gadd_prev] if gadd_prev else [], tag=f"F{jc}ga{t}")
                else:
                    ga = p.emit("vector", lambda e, t=t: e.tensor_add(
                        gsum[:, t:t + 1], gsum[:, t:t + 1], gt0 if t % 2 == 0 else gt1),
                        after=[ge], strict=[gadd_prev], tag=f"F{jc}ga{t}")
                gq = p.emit("scalar", lambda e, t=t: e.activation(
                    sqsc[:, (t % 2) * 512:(t % 2) * 512 + 512], self.hbuf[:, t * 512:(t + 1) * 512],
                    AF.Square, accum_out=gt0 if t % 2 == 0 else gt1),
                    after=[ge, ga], strict=[gsq_prev] if gsq_prev else [], tag=f"F{jc}q{t}")
                gsq_prev = gq
                if jc == 0:
                    gb = p.emit("vector", lambda e, t=t: e.tensor_copy(
                        g2[:, t:t + 1], gt0 if t % 2 == 0 else gt1), after=[gq], strict=[ga],
                        tag=f"F{jc}gb{t}")
                else:
                    gb = p.emit("vector", lambda e, t=t: e.tensor_add(
                        g2[:, t:t + 1], g2[:, t:t + 1], gt0 if t % 2 == 0 else gt1),
                        after=[gq], strict=[ga], tag=f"F{jc}gb{t}")
                gadd_prev = gb
                if t % 8 == 7:
                    yield
            hdma = p.dma("scalar", lambda e, jc=jc: e.dma_start(
                out=self.h_h.rearrange("(th tl) j -> tl th j", tl=128)[:, :, jc * 512:(jc + 1) * 512],
                in_=self.hbuf.rearrange("p (m l) -> p m l", l=512)),
                after=[hst_toks[(jc, NT - 1, "gelu")], gadd_prev], tag=f"Fh{jc}")
            hst_toks[(jc, "dma")] = hdma
            wld_prev[bu] = mm_last
        self.fc_state = (hst_toks, mm_last, gadd_prev, gsum, g2)
        yield

    def ffn_tail(self):
        p, st1, sqsc, psA = self.p, self.st1, self.sqsc, self.psA
        hst_toks, mm_last, gadd_prev, gsum, g2 = self.fc_state
        tr_toks = self.tr_toks
        finF = self.lnstats_finalize(gsum, g2, st1[:, 64:96], st1[:, 96:128], LN_EPS,
                                     [gadd_prev, gadd_prev], "Fst", n=4096)

        out_stores = []
        wld2_prev = {0: None, 1: None}
        af_prev = finF
        evy_prev = finF
        mmP_last = None
        mmP_prev_pc = None
        trh_prev = None
        ln4_sq_prev = None
        hld_prev = None
        ad_toks = {}
        ev_last = {}
        mm_slot = {}
        mm_slot_prev = {}
        ysb8 = self.ysb8

        def slot(tb8):
            return psA[:, tb8 * 512:tb8 * 512 + 256] if tb8 < 7 else self.psB[:, 0:256]

        for pc in range(4):
            trh = {}
            afs = {}
            for r in range(4):
                hlds = []
                for tbi in range(2):
                    tb8 = 2 * r + tbi
                    dep = [hld_prev] if hld_prev is not None else \
                        [hst_toks[(jj, "dma")] for jj in range(8)]
                    if r == 0 and trh_prev is not None:
                        dep += trh_prev
                    elif r > 0:
                        dep.append(trh[2 * r - 2])
                    hld = p.dma("gpsimd", lambda e, pc=pc, r=r, tbi=tbi: e.dma_start(
                        out=self.hbuf[:, tbi * 4096:(tbi + 1) * 4096],
                        in_=self.h_h.rearrange("(tt tl) j -> tt tl j", tl=128)[8 * pc + 2 * r + tbi]),
                        after=dep, tag=f"Ph{pc}_{r}_{tbi}")
                    hlds.append(hld)
                    hld_prev = hld
                    t = pc * 8 + tb8
                    afs[tb8] = p.emit("vector", lambda e, t=t, tbi=tbi: e.tensor_scalar(
                        self.hbuf[:, tbi * 4096:(tbi + 1) * 4096],
                        self.hbuf[:, tbi * 4096:(tbi + 1) * 4096],
                        st1[:, 64 + t:65 + t], st1[:, 96 + t:97 + t], ALU.mult, ALU.add),
                        after=[hld, finF], strict=[af_prev], tag=f"Pa{pc}_{tb8}")
                    af_prev = afs[tb8]
                trdep = [afs[2 * r], afs[2 * r + 1]]
                if pc == 0:
                    trdep.append(mm_last)
                else:
                    trdep += [mm_slot_prev[2 * r], mm_slot_prev[2 * r + 1]]
                tr = p.dma("sync", lambda e, r=r: e.dma_start(
                    out=self.pool[:, r * 8192:(r + 1) * 8192].rearrange("p (m l) -> p m l", l=128),
                    in_=self.hbuf[:, 0:8192], transpose=True),
                    after=trdep, xpose=True, tag=f"Pt{pc}_{r}")
                trh[2 * r] = trh[2 * r + 1] = tr
            evs = {}
            for dc in range(4):
                wbu = dc % 2
                wdep = ([wld2_prev[wbu]] if wld2_prev[wbu] is not None else
                        self.tok_quant + (tr_toks[-1:] if wbu == 0 else [mm_last]))
                wld2a = p.dma("scalar", lambda e, dc=dc, wbu=wbu: e.dma_start(
                    out=self.wst_pr[wbu].rearrange("p (jh d) -> p jh d", jh=32)[:, 0:16],
                    in_=self.wprq_h.rearrange("(jh jl) d -> jl jh d", jl=128)
                        [:, 0:16, dc * 256:(dc + 1) * 256]), after=wdep, tag=f"Pw{pc}_{dc}a")
                wld2b = p.dma("gpsimd", lambda e, dc=dc, wbu=wbu: e.dma_start(
                    out=self.wst_pr[wbu].rearrange("p (jh d) -> p jh d", jh=32)[:, 16:32],
                    in_=self.wprq_h.rearrange("(jh jl) d -> jl jh d", jl=128)
                        [:, 16:32, dc * 256:(dc + 1) * 256]), after=wdep, tag=f"Pw{pc}_{dc}b")
                for tb8 in range(8):
                    mms = []
                    for jh in range(32):
                        dep = [wld2a if jh < 16 else wld2b, trh[tb8]]
                        if jh == 0:
                            prev_ev = ev_last.get(tb8)
                            if prev_ev is not None:
                                dep.append(prev_ev)
                            if pc == 0 and dc == 0:
                                dep += self.tok_quant
                                if tb8 < 2:
                                    dep += [hst_toks[(7, NT - 1, "gelu")],
                                            hst_toks[(7, NT - 2, "gelu")]]
                                if tb8 == 7:
                                    dep += self.psB_chain
                        mms.append(p.emit("tensor", lambda e, jh=jh, tb8=tb8, dc=dc,
                                          wbu=wbu: e.matmul(
                            slot(tb8),
                            self.hT8[tb8][:, jh * 128:(jh + 1) * 128],
                            self.wst_pr[wbu][:, jh * 256:(jh + 1) * 256],
                            start=(jh == 0), stop=(jh == 31)), after=dep,
                            tag=f"P{pc}m{dc}_{tb8}_{jh}"))
                    mmP_last = mms[-1]
                    mm_slot[tb8] = mms[-1]
                    evdep = [mms[-1]]
                    if pc == 0 and dc == 0 and tb8 == 0:
                        evdep += [hst_toks[(7, "dma")], finF]
                    ev = p.emit("vector", lambda e, tb8=tb8, dc=dc: e.tensor_add(
                        ysb8[:, tb8 * 1024 + dc * 256:tb8 * 1024 + (dc + 1) * 256],
                        slot(tb8), self.ybB_t[:, dc * 256:(dc + 1) * 256]),
                        after=evdep, strict=[evy_prev], tag=f"Pe{pc}_{dc}_{tb8}")
                    evy_prev = ev
                    evs[(dc, tb8)] = ev
                    ev_last[tb8] = ev
                wld2_prev[wbu] = mmP_last
            mmP_prev_pc = mmP_last
            mm_slot_prev = dict(mm_slot)
            trh_prev = [trh[0], trh[2], trh[4], trh[6]]
            s8, q8 = self.gst[:, 0:8], self.gst[:, 8:16]
            mu8, v8, t8 = self.gst[:, 16:24], self.gst[:, 24:32], self.gst[:, 32:40]
            rstd8, nmu8 = self.gst[:, 40:48], self.gst[:, 48:56]
            sstats = []
            for tb8 in range(8):
                sm = p.emit("vector", lambda e, tb8=tb8: e.reduce_sum(
                    s8[:, tb8:tb8 + 1], ysb8[:, tb8 * 1024:(tb8 + 1) * 1024], axis=AX.X),
                    after=[evs[(3, tb8)]], strict=[evs[(3, tb8)]], tag=f"P4s{pc}_{tb8}")
                sq = p.emit("scalar", lambda e, tb8=tb8: e.activation(
                    sqsc[:, 1024:2048], ysb8[:, tb8 * 1024:(tb8 + 1) * 1024], AF.Square,
                    accum_out=q8[:, tb8:tb8 + 1]),
                    after=[evs[(3, tb8)]], strict=[ln4_sq_prev] if ln4_sq_prev else [],
                    tag=f"P4q{pc}_{tb8}")
                ln4_sq_prev = sq
                sstats += [sm, sq]
            m4 = p.emit("vector", lambda e: e.tensor_scalar(mu8, s8, 1.0 / D, None,
                                                            ALU.mult), after=sstats,
                        strict=sstats[-2:], tag=f"P4m{pc}")
            q4 = p.emit("vector", lambda e: e.tensor_mul(v8, mu8, mu8), strict=[m4], tag=f"P4v{pc}")
            q5 = p.emit("vector", lambda e: e.tensor_scalar(t8, q8, 1.0 / D, LN_EPS,
                                                            ALU.mult, ALU.add),
                        strict=[q4], tag=f"P4w{pc}")
            q6 = p.emit("vector", lambda e: e.tensor_sub(v8, t8, v8), strict=[q5], tag=f"P4u{pc}")
            nr4 = self.newton_rsqrt(rstd8, v8, 8, [q6], tagp=f"P4{pc}")
            nm4 = p.emit("vector", lambda e: e.tensor_mul(nmu8, mu8, rstd8),
                         strict=[nr4], tag=f"P4n{pc}")
            nm4b = p.emit("vector", lambda e: e.tensor_scalar(nmu8, nmu8, -1.0, None, ALU.mult),
                          strict=[nm4], tag=f"P4nb{pc}")
            for tb8 in range(8):
                t = pc * 8 + tb8
                bu = t % 2
                lx = p.dma("sync", lambda e, t=t, bu=bu: e.dma_start(
                    out=self.x1s2_f[bu], in_=self.x1_h[t * 128:(t + 1) * 128, :]),
                    after=[self.x1_store[t]] + ([out_stores[-2]] if len(out_stores) >= 2
                                                else [nm4b, mmP_last]),
                    tag=f"Px{t}")
                ln4 = p.emit("vector", lambda e, tb8=tb8, bu=bu: e.tensor_scalar(
                    sqsc[:, bu * 1024:(bu + 1) * 1024], ysb8[:, tb8 * 1024:(tb8 + 1) * 1024],
                    rstd8[:, tb8:tb8 + 1], nmu8[:, tb8:tb8 + 1], ALU.mult, ALU.add),
                    after=[nm4b], strict=[evy_prev], tag=f"Pl{t}")
                evy_prev = ln4
                ad = p.emit("vector", lambda e, bu=bu: e.tensor_add(
                    self.outstP[:, bu * 1024:(bu + 1) * 1024], sqsc[:, bu * 1024:(bu + 1) * 1024],
                    self.x1s2_f[bu]),
                    after=[lx], strict=[ln4], tag=f"Po{t}")
                ad_toks[t] = ad
                st_ = p.dma("sync", lambda e, t=t, bu=bu: e.dma_start(
                    out=self.out_h[t * 128:(t + 1) * 128, :],
                    in_=self.outstP[:, bu * 1024:(bu + 1) * 1024]), after=[ad], tag=f"Ps{t}")
                out_stores.append(st_)
        self.out_stores = out_stores

    def build(self, stage="full"):
        self.hstate = {}
        self.psB_chain = []
        self.ln1T_tok = [None, None]
        self.ln1T_users = [[], []]
        self.ln1c_users = [[], []]
        self.chunk_half_done = {}
        self.ust_toks = []
        self.den_toks = []
        self.phase0()
        self.quant_prep()
        self.hstate["ctx_init"] = [self.tok_lncols]
        qgen = self.quant_units()
        self.phase1()

        def pump(n):
            for _ in range(n):
                try:
                    next(qgen)
                except StopIteration:
                    break
        self.phase1_pump = pump

        def drain(g):
            for _ in g:
                pass
        self.chunk(0)
        drain(self.prep_gen(0))
        for h in range(H):
            qg = self.qc_gen(h)
            pg = None
            if h + 1 < H:
                if (h + 1) % 2 == 0:
                    self.chunk((h + 1) // 2)
                pg = self.prep_gen(h + 1)
            nq = 0
            for _ in qg:
                nq += 1
                for _ in range(2 if nq <= 20 else 1):
                    if pg is None:
                        break
                    try:
                        next(pg)
                    except StopIteration:
                        pg = None
                if nq % 28 == 0:
                    pump(1)
            if pg is not None:
                drain(pg)
            pump(2)
        pump(1000)
        if stage == "attn":
            self.p.emit("sync", lambda e: e.nop(), after=self.ust_toks + self.den_toks, tag="fin")
            self.p.build().close()
            return self.nc, self.dbg
        p4g = self.phase4_gen()
        if stage in ("x1", "quant"):
            drain(p4g)
            fin = self.x1_store if stage == "x1" else self.tok_quant + self.x1_store
            self.p.emit("sync", lambda e: e.nop(), after=fin, tag="fin")
            self.p.build().close()
            return self.nc, self.dbg
        next(p4g)
        for _ in range(3):
            next(p4g)
        fcg = self.fc_gen()
        for g in range(1, 4):
            for _ in range(3):
                next(p4g)
            next(fcg)
        drain(p4g)
        drain(fcg)
        self.ffn_tail()
        self.p.emit("sync", lambda e: e.nop(), after=self.out_stores, tag="fin")
        self.p.build().close()
        return self.nc, self.dbg


def build_nc(stage=None, taps=None):
    mk = MK(taps if taps is not None else DEBUG_TAPS)
    return mk.build(stage if stage is not None else DEBUG_STAGE)


def kernel(L0_attn_x, L0_attn_W_DQ, L0_attn_W_UQ, L0_attn_W_DKV, L0_attn_W_UKV,
           L0_attn_W_QR, L0_attn_W_KR, L0_attn_W_O, L0_attn_q_norm_w,
           L0_attn_k_norm_w, L0_ffn_W_fc, L0_ffn_ln_w, L0_ffn_ln_b, L0_ffn_W_proj):
    x = np.ascontiguousarray(np.asarray(L0_attn_x, dtype=np.float32))
    nc, _ = build_nc()
    common = {
        "W_DQ": np.asarray(L0_attn_W_DQ, np.float32), "W_UQ": np.asarray(L0_attn_W_UQ, np.float32),
        "W_DKV": np.asarray(L0_attn_W_DKV, np.float32),
        "W_UKV": np.asarray(L0_attn_W_UKV, np.float32),
        "W_QR": np.asarray(L0_attn_W_QR, np.float32), "W_KR": np.asarray(L0_attn_W_KR, np.float32),
        "W_O": np.asarray(L0_attn_W_O, np.float32),
        "q_norm_w": np.asarray(L0_attn_q_norm_w, np.float32),
        "k_norm_w": np.asarray(L0_attn_k_norm_w, np.float32),
        "W_fc": np.asarray(L0_ffn_W_fc, np.float32), "ln_w": np.asarray(L0_ffn_ln_w, np.float32),
        "ln_b": np.asarray(L0_ffn_ln_b, np.float32), "W_proj": np.asarray(L0_ffn_W_proj, np.float32),
    }
    in_maps = [dict(common, x=np.ascontiguousarray(x[b])) for b in range(8)]
    res = run_bass_kernel_spmd(nc, in_maps, core_ids=list(range(8)))
    out = np.stack([res.results[b]["out"] for b in range(8)], axis=0)
    return out.astype(np.float32)
